# version 17
# baseline (speedup 1.0000x reference)
import numpy as np
import ml_dtypes
import concourse.bass as bass
import concourse.mybir as mybir
from concourse.bass_utils import run_bass_kernel_spmd
from contextlib import ExitStack

F32 = mybir.dt.float32
BF16 = mybir.dt.bfloat16
I32 = mybir.dt.int32
AF = mybir.ActivationFunctionType
ALU = mybir.AluOpType
AX = mybir.AxisListType.X

P = 128
D = 2048
ND = 16
II = 8192
NI = 64
TCORE = 4096
TC = 512
NCH = 8
NT = 4
EPS = 1e-5


class _StubInstr:
    def then_inc(self, *a, **k):
        return self


class _StubEngine:
    def __getattr__(self, name):
        def f(*a, **k):
            return _StubInstr()
        return f


class _EvDict(dict):
    pass2 = False

    def __missing__(self, key):
        if self.pass2:
            raise KeyError(key)
        return 0


def build_nc():
    nc = bass.Bass("TRN2", target_bir_lowering=False)

    x_in = nc.dram_tensor("x", [TCORE, D], F32, kind="ExternalInput").ap()
    av = nc.dram_tensor("av", [1, D], F32, kind="ExternalInput").ap()
    bv = nc.dram_tensor("bv", [1, D], F32, kind="ExternalInput").ap()
    cv = nc.dram_tensor("cv", [1, D], F32, kind="ExternalInput").ap()
    dv = nc.dram_tensor("dv", [1, D], F32, kind="ExternalInput").ap()
    gbv = nc.dram_tensor("gbv", [1, D], F32, kind="ExternalInput").ap()
    dscv = nc.dram_tensor("dscv", [1, D], F32, kind="ExternalInput").ap()
    gscv = nc.dram_tensor("gscv", [1, II], F32, kind="ExternalInput").ap()
    uscv = nc.dram_tensor("uscv", [1, II], F32, kind="ExternalInput").ap()
    gate_w = nc.dram_tensor("gate_w", [D, D], F32, kind="ExternalInput").ap()
    gq = nc.dram_tensor("gq", [II, D], I32, kind="ExternalInput").ap()
    uq = nc.dram_tensor("uq", [II, D], I32, kind="ExternalInput").ap()
    dq = nc.dram_tensor("dq", [D, II], I32, kind="ExternalInput").ap()
    dlru = nc.dram_tensor("dlru", [D, 4], F32, kind="ExternalInput").ap()
    dlrv = nc.dram_tensor("dlrv", [4, D], F32, kind="ExternalInput").ap()
    identf = nc.dram_tensor("identf", [P, P], F32, kind="ExternalInput").ap()
    identb = nc.dram_tensor("identb", [P, P], BF16, kind="ExternalInput").ap()
    masks = nc.dram_tensor("masks", [1, 64], F32, kind="ExternalInput").ap()

    out_d = nc.dram_tensor("out", [TCORE, D], F32, kind="ExternalOutput").ap()

    wgs = nc.dram_tensor("wgs", [ND, P, D], BF16).ap()
    wgfs = nc.dram_tensor("wgfs", [NI, P, D], BF16).ap()
    wufs = nc.dram_tensor("wufs", [NI, P, D], BF16).ap()
    wdfs = nc.dram_tensor("wdfs", [ND, P, II], BF16).ap()
    ugs = nc.dram_tensor("ugs", [NCH, P, ND * TC], BF16).ap()
    gfs = nc.dram_tensor("gfs", [NCH, P, ND * TC], F32).ap()
    ccin_ap = nc.dram_tensor("ccin", [2, D], F32).ap()
    ccout_ap = nc.dram_tensor("ccout", [16, D], F32).ap()

    with ExitStack() as ctx:
        e = ctx.enter_context

        h = e(nc.sbuf_tensor("h", [P, NI * TC], BF16))
        ug = e(nc.sbuf_tensor("ug", [P, ND * TC], BF16))
        xnT = e(nc.sbuf_tensor("xnT", [P, ND * TC], BF16))
        x_rot = [e(nc.sbuf_tensor(f"xr{i}", [P, D], F32)) for i in range(2)]
        tok_rot = [e(nc.sbuf_tensor(f"tr{i}", [P, D], F32)) for i in range(2)]
        wstr_g = [e(nc.sbuf_tensor(f"wg{i}", [P, 1024], BF16)) for i in range(2)]
        wstr_u = [e(nc.sbuf_tensor(f"wu{i}", [P, D], BF16)) for i in range(2)]
        wstr_d = [e(nc.sbuf_tensor(f"wd{i}", [P, 4096], BF16)) for i in range(2)]
        wdc = e(nc.sbuf_tensor("wdc", [P, 4096], BF16))
        wes = e(nc.sbuf_tensor("wes", [P, 4096], BF16))
        fgh = [e(nc.sbuf_tensor(f"fh{i}", [P, 1024], BF16)) for i in range(4)]
        sg_rot = [e(nc.sbuf_tensor(f"sg{i}", [P, TC], F32)) for i in range(2)]
        gf_rot = [e(nc.sbuf_tensor(f"gf{i}", [P, TC], F32)) for i in range(2)]
        y_rot = sg_rot
        fghF = [f.bitcast(F32) for f in fgh]

        idf = e(nc.sbuf_tensor("idf", [P, P], F32))
        idb = e(nc.sbuf_tensor("idb", [P, P], BF16))
        cols = e(nc.sbuf_tensor("cols", [P, 16 * 11], F32))
        gscc = e(nc.sbuf_tensor("gscc", [P, NI], F32))
        uscc = e(nc.sbuf_tensor("uscc", [P, NI], F32))
        dlruc = e(nc.sbuf_tensor("dlruc", [P, 64], F32))
        bigB = e(nc.sbuf_tensor("bigB", [P, 3 * 512], F32))
        bigBI = bigB.bitcast(I32)
        maskc = e(nc.sbuf_tensor("maskc", [1, 64], F32))
        onesr = e(nc.sbuf_tensor("onesr", [1, P], F32))
        qm = e(nc.sbuf_tensor("qm", [P, 64], F32))
        jk16 = e(nc.sbuf_tensor("jk16", [P, 16], F32))
        pwa = e(nc.sbuf_tensor("pwa", [P, 16], F32))
        pwb = e(nc.sbuf_tensor("pwb", [P, 16], F32))
        a512 = e(nc.sbuf_tensor("a512", [P, 16], F32))
        pw1 = e(nc.sbuf_tensor("pw1", [P, 16], F32))
        pw2 = e(nc.sbuf_tensor("pw2", [P, 16], F32))
        sin_t = e(nc.sbuf_tensor("sin_t", [P, 16], F32))
        cura = e(nc.sbuf_tensor("cura", [P, 16], F32))
        curb = e(nc.sbuf_tensor("curb", [P, 16], F32))
        rsb = e(nc.sbuf_tensor("rsb", [4, 1], F32))
        carr = e(nc.sbuf_tensor("carr", [P, 16], F32))
        carrh = e(nc.sbuf_tensor("carrh", [P, 16 * 9], F32))
        sf = e(nc.sbuf_tensor("sf", [P, 16 * 8], F32))
        convb = e(nc.sbuf_tensor("convb", [P, 2 * D], BF16))
        convbF = convb.bitcast(F32)
        vrow = convbF[0:1, 0:D]
        usall = e(nc.sbuf_tensor("usall", [P, 128], F32))
        lnst = e(nc.sbuf_tensor("lnst", [P, 16], F32))
        lnbn = e(nc.sbuf_tensor("lnbn", [P, 2 * 24], F32))
        epst = e(nc.sbuf_tensor("epst", [P, 1], F32))
        m128t = e(nc.sbuf_tensor("m128t", [P, 1], F32))
        carrT = convbF[0:16, 0:P]
        ustotT = convbF[0:16, P:2 * P]
        gcol = convbF[32:48, :]
        dlrvc = convbF[0:4, :]

        wd_bufs = [wstr_d[0][:, :], wstr_d[1][:, :], wdc[:, :], wes[:, :]]

        psF = [e(nc.psum_tensor(f"psF{i}", [P, TC], F32)) for i in range(6)]
        psT = [e(nc.psum_tensor(f"psT{i}", [P, 2 * TC], BF16)) for i in range(2)]

        sems = {k: e(nc.semaphore(f"s_{k}")) for k in
                ("pe", "dve", "act", "cc",
                 "cst", "vrw", "ax0", "ax1",
                 "wi0", "wi1", "wi2",
                 "wg0", "wg1", "wu0", "wu1",
                 "wd0", "wd1", "wd2", "wd3",
                 "fg0", "fg1", "fg2", "fg3",
                 "gs0", "gs1", "gfl0", "gfl1", "gfl2", "gfl3",
                 "ugl", "ugl1", "cci", "ccg",
                 "ust", "wo0", "wo1", "xpr",
                 "os0", "os1", "os2", "os3", "os4", "os5", "os6", "os7")}
        block = e(nc.Block())
        n = {k: 0 for k in sems}
        E = _EvDict()

        def tick(key, instr, k=1):
            instr.then_inc(sems[key], k)
            n[key] += k
            return (key, n[key])

        def ev(key):
            return (key, n[key])

        def wte(eng, evt):
            if evt:
                eng.wait_ge(sems[evt[0]], evt[1])

        CA, CB, CC, CD, CGB, CDSC, CCB, CDLR, CDEFF, CUM, CUST = range(11)

        def col(idx, j=None):
            if j is None:
                return cols[:, idx * 16:(idx + 1) * 16]
            return cols[:, idx * 16 + j: idx * 16 + j + 1]

        sections = []

        class _SelfSync:

            def __init__(self, eng, key):
                self._eng = eng
                self._key = key
                self._last = -1

            def wait_ge(self, *a, **k):
                return self._eng.wait_ge(*a, **k)

            def __getattr__(self, name):
                real = getattr(self._eng, name)

                def f(*a, **k):
                    if n[self._key] > 0 and n[self._key] != self._last:
                        self._eng.wait_ge(sems[self._key], n[self._key])
                        self._last = n[self._key]
                    return real(*a, **k)
                return f

        _selfsync_key = {"vector": "dve", "scalar": "act"}

        def sect(engname, fn):
            if engname in _selfsync_key:
                key = _selfsync_key[engname]
                orig = fn

                def wrapped(eng, _orig=orig, _key=key):
                    return _orig(_SelfSync(eng, _key))
                sections.append((engname, wrapped))
            else:
                sections.append((engname, fn))

        NVEC = 14

        def pre_sync(s):
            tick("cst", s.dma_start(out=idf[:], in_=identf[:]), 16)
            tick("cst", s.dma_start(out=idb[:], in_=identb[:]), 16)
            tick("cst", s.dma_start(out=maskc[:], in_=masks[:]), 16)
            tick("cst", s.dma_start(
                out=dlruc.ap().rearrange("p (j r) -> p j r", j=ND),
                in_=dlru.rearrange("(j p) r -> p j r", p=P)), 16)
            E["pre_const"] = ev("cst")
            E["xpr"] = tick("xpr", s.dma_start(out=out_d[:, :], in_=x_in[:, :]), 16)
            for i in range(NVEC):
                if i >= 1:
                    wte(s, E[("vtp", i - 1)])
                if i < 6:
                    src = [av, bv, cv, dv, gbv, dscv][i][:]
                elif i < 10:
                    src = gscv[0:1, (i - 6) * D:(i - 5) * D]
                else:
                    src = uscv[0:1, (i - 10) * D:(i - 9) * D]
                E[("vrow", i)] = tick("vrw", s.dma_start(out=vrow[:], in_=src), 16)
        sect("sync", pre_sync)

        def pre_pe(t):
            wte(t, E["pre_const"])
            for i in range(NVEC):
                wte(t, E[("vrow", i)])
                if i >= 2:
                    wte(t, E[("vev", i - 2)])
                for k in range(16):
                    ins = t.transpose(psF[i % 2][:, k:k + 1],
                                      vrow[0:1, k * P:(k + 1) * P],
                                      idf[0:1, 0:1])
                E[("vtp", i)] = tick("pe", ins)
        sect("tensor", pre_pe)

        def pre_dve(v):
            for i in range(NVEC):
                wte(v, E[("vtp", i)])
                if i < 6:
                    dst = col([CA, CB, CC, CD, CGB, CDSC][i])
                elif i < 10:
                    dst = gscc[:, (i - 6) * 16:(i - 5) * 16]
                else:
                    dst = uscc[:, (i - 10) * 16:(i - 9) * 16]
                E[("vev", i)] = tick("dve", v.tensor_copy(dst, psF[i % 2][:, 0:16]))
            tick("dve", v.memset(carr[:], 0.0))
            tick("dve", v.memset(carrh[:, 0:16], 0.0))
            tick("dve", v.memset(onesr[:], 1.0))
            tick("dve", v.memset(epst[:], EPS))
            tick("dve", v.memset(m128t[:], -128.0))
            E["pre_dve"] = tick("dve", v.tensor_tensor(col(CCB), col(CB), col(CC),
                                                       op=ALU.mult))
        sect("vector", pre_dve)

        def pre_act(a):
            wte(a, E[("vev", 0)])
            E["tanh1"] = tick("act", a.activation(lnst[:, 0:16], col(CA), AF.Tanh))
        sect("scalar", pre_act)

        def pre_dve2(v):
            wte(v, E["tanh1"])
            E["aacol"] = tick("dve", v.tensor_copy(col(CA), lnst[:, 0:16]))
        sect("vector", pre_dve2)

        aac = col(CA)

        mats = [
            dict(src=gate_w, scr=wgs, tiles=ND, wid=D, int_=False),
            dict(src=gq, scr=wgfs, tiles=NI, wid=D, int_=True),
            dict(src=uq, scr=wufs, tiles=NI, wid=D, int_=True),
            dict(src=dq, scr=wdfs, tiles=ND, wid=II, int_=True),
        ]
        _sb = 0
        for mi, m in enumerate(mats):
            m["segbase"] = _sb
            m["nseg"] = m["wid"] // D
            _sb += m["tiles"] * m["nseg"]
        NSEG_ALL = _sb

        worder = list(range(16))
        for j in range(NI):
            worder.append(16 + j)
            worder.append(16 + NI + j)
        worder.extend(range(16 + 2 * NI, NSEG_ALL))
        assert len(worder) == NSEG_ALL

        def seg_info(si):
            for m in mats:
                if m["segbase"] <= si < m["segbase"] + m["tiles"] * m["nseg"]:
                    rel = si - m["segbase"]
                    return m, rel // m["nseg"], rel % m["nseg"]
            raise AssertionError(si)

        def w_win(s, oi):
            si = worder[oi]
            m, j, seg = seg_info(si)
            stg = bigB if not m["int_"] else bigBI
            for g in range(4):
                qi = oi * 4 + g
                if qi >= 3:
                    wte(s, E[("wcq", (qi - 3) // 4, (qi - 3) % 4)])
                E[("win", oi, g)] = tick(
                    "wi%d" % (qi % 3),
                    s.dma_start(out=stg[:, (qi % 3) * 512:(qi % 3 + 1) * 512],
                                in_=m["src"][j * P:(j + 1) * P,
                                             seg * D + g * 512:
                                             seg * D + (g + 1) * 512]), 16)

        def w_conv(eng, key, oi):
            si = worder[oi]
            m, j, seg = seg_info(si)
            stg = bigB if not m["int_"] else bigBI
            for g in range(4):
                qi = oi * 4 + g
                wte(eng, E[("win", oi, g)])
                if oi >= 2:
                    wte(eng, E[("wtp", (oi - 2) * 4 + g)])
                if oi == 0 and g == 0:
                    wte(eng, E["pre_dve"])
                srcap = stg[:, (qi % 3) * 512:(qi % 3 + 1) * 512]
                dst = convb[:, (oi % 2) * D + g * 512:
                            (oi % 2) * D + (g + 1) * 512]
                if m["int_"]:
                    if key == "dve":
                        ins = eng.tensor_scalar(dst, srcap, -128.0, None,
                                                op0=ALU.add)
                    else:
                        ins = eng.activation(dst, srcap, AF.Prelu,
                                             bias=m128t[:, 0:1], alpha=1.0)
                else:
                    ins = eng.tensor_copy(dst, srcap)
                E[("wcq", oi, g)] = tick(key, ins)

        def w_tp(t, oi):
            for g in range(4):
                gi = oi * 4 + g
                wte(t, E[("wcq", oi, g)])
                if gi >= 2:
                    wte(t, E[("wev", gi - 2)])
                for q in range(4):
                    kb = g * 4 + q
                    ins = t.transpose(
                        psT[gi % 2][:, q * P:(q + 1) * P],
                        convb[:, (oi % 2) * D + kb * P:
                              (oi % 2) * D + (kb + 1) * P],
                        idb[:])
                E[("wtp", gi)] = tick("pe", ins)

        def w_ev(eng, key, oi, g):
            gi = oi * 4 + g
            wte(eng, E[("wtp", gi)])
            if oi >= 2:
                wte(eng, E[("woutO", oi - 2)])
            pb = wes[:, (oi % 2) * D: (oi % 2) * D + D]
            if key == "dve":
                ins = eng.tensor_copy(pb[:, g * TC:(g + 1) * TC],
                                      psT[gi % 2][:, 0:TC])
            else:
                ins = eng.activation(pb[:, g * TC:(g + 1) * TC],
                                     psT[gi % 2][:, 0:TC], AF.Copy)
            E[("wev", gi)] = tick(key, ins)

        def w_out(p, oi):
            si = worder[oi]
            m, j, seg = seg_info(si)
            for g in range(4):
                wte(p, E[("wev", oi * 4 + g)])
            pb = wes[:, (oi % 2) * D: (oi % 2) * D + D]
            evt = tick("wo%d" % (oi % 2),
                       p.dma_start(out=m["scr"][j][:, seg * D:(seg + 1) * D],
                                   in_=pb), 16)
            E[("wout", si)] = evt
            E[("woutO", oi)] = evt
            if oi == NSEG_ALL - 1:
                E["W_spl0"] = ev("wo0")
                E["W_spl1"] = ev("wo1")

        def _w_dve(v, oi):
            if oi < ND or oi % 2 == 0:
                w_conv(v, "dve", oi)
            for g in (0, 2):
                w_ev(v, "dve", oi, g)

        def _w_act(a, oi):
            if oi >= ND and oi % 2 == 1:
                w_conv(a, "act", oi)
            for g in (1, 3):
                w_ev(a, "act", oi, g)

        _w_emit = {"dve": _w_dve, "act": _w_act, "pe": w_tp,
                   "pli": w_win, "plo": w_out}
        wcur = {}

        def w_take(eng, which, k, stop):
            end = min(wcur[which] + k, stop)
            while wcur[which] < end:
                _w_emit[which](eng, wcur[which])
                wcur[which] += 1

        def w_take_until(eng, which, until, stop):
            end = min(until, stop)
            while wcur[which] < end:
                _w_emit[which](eng, wcur[which])
                wcur[which] += 1

        W_STOP = [NSEG_ALL] * NCH

        sect("vector", lambda v: w_take(v, "dve", 16, 16))
        sect("tensor", lambda t: w_take(t, "pe", 16, 16))
        sect("scalar", lambda a: w_take(a, "act", 16, 16))

        def pre_xload(p):
            for g in (0, 1):
                E[("a_xl", g)] = tick("ax%d" % (g % 2), p.dma_start(
                    out=x_rot[g % 2][:],
                    in_=x_in[g * P:(g + 1) * P, :]), 16)
        sect("gpsimd", pre_xload)
        def w_prologue_pl(p):
            for i in range(16):
                w_take(p, "pli", 1, 16)
                if i >= 2:
                    w_take(p, "plo", 1, 14)
            w_take(p, "plo", 2, 16)
        sect("gpsimd", w_prologue_pl)

        def a_wg_load(s, c, j):
            for hh in range(2):
                jj = c * ND + j
                if jj >= 1:
                    wte(s, E[("a_ggh", (jj - 1) // ND, (jj - 1) % ND, hh)])
                if c == 0:
                    wte(s, E[("wout", j)])
                E[("a_wg", c, j, hh)] = tick(
                    "wg%d" % hh,
                    s.dma_start(out=wstr_g[hh][:],
                                in_=wgs[j][:, hh * 1024:(hh + 1) * 1024]),
                    16)

        def a_sync(s, c):
            for j in range(ND):
                a_wg_load(s, c, j)

        def a_act_sqrt(a, c, half):
            g1 = c * NT + 2 * half + 1
            wte(a, E[("a_ag", g1)])
            E[("a_sd", c, half)] = tick("act", a.activation(
                lnst[:, 8 + 2 * half: 10 + 2 * half],
                lnst[:, 4 * half + 1: 4 * half + 4: 2], AF.Sqrt,
                bias=epst[:, 0:1]))

        def sgslot(j):
            return tok_rot[1][:, (j % 4) * TC:((j % 4) + 1) * TC]

        def a_act_sig(a, c, j):
            wte(a, E[("a_gg", c, j)])
            if j < 4:
                wte(a, E[("a_tpl", c * NT + 3)])
                if c > 0:
                    wte(a, E[("a_ug", c - 1, 12 + j)])
            else:
                wte(a, E[("a_ug", c, j - 4)])
            E[("a_sig", c, j)] = tick("act", a.activation(
                sgslot(j), psF[j % 4][:], AF.Sigmoid,
                bias=col(CGB, j)))

        def a_pe_tp(t, c, half=None):
            rng = range(NT) if half is None else range(2 * half, 2 * half + 2)
            for sft in rng:
                if c >= 1:
                    w_take(t, "pe", 2, NSEG_ALL)
                g = c * NT + sft
                wte(t, E[("a_xn", g)])
                for grp in range(4):
                    if sft == 0 and c > 0:
                        if grp == 0:
                            wte(t, E[("c_sg", c - 1, 62)])
                        elif grp == 1:
                            wte(t, E[("c_h", c - 1, 62)])
                        elif grp == 2:
                            wte(t, E[("c_sg", c - 1, 63)])
                        else:
                            wte(t, E[("c_h", c - 1, 63)])
                    elif sft > 0:
                        wte(t, E[("a_tev", g - 1, grp)])
                    for q in range(4):
                        kb = grp * 4 + q
                        ins = t.transpose(
                            psF[grp][:, q * P:(q + 1) * P],
                            tok_rot[sft % 2][:, kb * P:(kb + 1) * P],
                            idf[:])
                    E[("a_tp", g, grp)] = tick("pe", ins)
                E[("a_tpl", g)] = ev("pe")

        def a_pe_gg(t, c, j):
            if j == 0:
                wte(t, E[("a_tevl", c)])
            if j < 4:
                wte(t, E[("a_tev", c * NT + 3, j % 4)])
            else:
                wte(t, E[("a_sig", c, j - 4)])
            for hh in range(2):
                wte(t, E[("a_wg", c, j, hh)])
                for k in range(8 * hh, 8 * hh + 8):
                    ins = t.matmul(
                        psF[j % 4][:],
                        wstr_g[hh][:, (k - 8 * hh) * P:(k - 8 * hh + 1) * P],
                        xnT[:, k * TC:(k + 1) * TC],
                        start=(k == 0), stop=(k == ND - 1))
                E[("a_ggh", c, j, hh)] = tick("pe", ins)
            E[("a_gg", c, j)] = ("pe", n["pe"])

        def a_pe(t, c):
            a_pe_tp(t, c)
            for j in range(ND):
                a_pe_gg(t, c, j)

        def a_ln_stats(v, c, half):
            for i2 in range(2):
                sft = 2 * half + i2
                g = c * NT + sft
                wte(v, E[("a_xl", g)])
                for qq in range(4):
                    tick("dve", v.bn_stats(
                        lnbn[:, sft % 2 * 24 + qq * 6:
                             sft % 2 * 24 + (qq + 1) * 6],
                        x_rot[sft % 2][:, qq * TC:(qq + 1) * TC]))
                E[("a_ag", g)] = tick("dve", v.bn_aggr(
                    lnst[:, 4 * half + 2 * i2: 4 * half + 2 * i2 + 2],
                    lnbn[:, sft % 2 * 24: sft % 2 * 24 + 24]))

        def a_ln_norm(v, c, half):
            wte(v, E[("a_sd", c, half)])
            tick("dve", v.reciprocal(lnst[:, 8 + 2 * half: 10 + 2 * half],
                                     lnst[:, 8 + 2 * half: 10 + 2 * half]))
            for i2 in range(2):
                sft = 2 * half + i2
                g = c * NT + sft
                if g >= 2:
                    wte(v, E[("a_tpl", g - 2)])
                E[("a_xn", g)] = tick("dve", v.tensor_scalar(
                    tok_rot[sft % 2][:], x_rot[sft % 2][:],
                    lnst[:, 4 * half + 2 * i2: 4 * half + 2 * i2 + 1],
                    lnst[:, 8 + sft: 8 + sft + 1],
                    op0=ALU.subtract, op1=ALU.mult))

        def a_ln_pair(v, c, half):
            a_ln_stats(v, c, half)
            a_ln_norm(v, c, half)

        def a_evacs_half(v, c, half):
            for sft in range(2 * half, 2 * half + 2):
                if c >= 1:
                    w_take(v, "dve", 2, NSEG_ALL)
                g = c * NT + sft
                for grp in range(4):
                    wte(v, E[("a_tp", g, grp)])
                    if c > 0 and sft == 0 and grp == 0:
                        wte(v, E[("a_gg", c - 1, ND - 1)])
                        wte(v, E[("c_gu", c - 1, NI - 1)])
                    dst = xnT.ap().rearrange("p (k t) -> p k t", k=ND)[
                        :, grp * 4:(grp + 1) * 4, sft * P:(sft + 1) * P]
                    src = psF[grp].ap().rearrange(
                        "p (q u) -> p q u", q=4)
                    E[("a_tev", g, grp)] = tick("dve", v.tensor_copy(dst, src))
            if half == 1:
                E[("a_tevl", c)] = ev("dve")

        def a_ug_scan(v, c):
            if c == 1:
                w_take(v, "dve", NSEG_ALL, NSEG_ALL)
            for j in range(ND):
                if c >= 1:
                    w_take(v, "dve", 4, NSEG_ALL)
                wte(v, E[("a_sig", c, j)])
                if c > 0 and j == 0:
                    wte(v, E[("a_ugst", c - 1)])
                E[("a_ug", c, j)] = tick("dve", v.scalar_tensor_tensor(
                    ug[:, j * TC:(j + 1) * TC],
                    xnT[:, j * TC:(j + 1) * TC], 1.0, sgslot(j),
                    op0=ALU.mult, op1=ALU.mult,
                    accum_out=usall[:, j * 8 + c: j * 8 + c + 1]))
            E[("a_ugl", c)] = ev("dve")
            wte(v, E[("a_tpl", c * NT + 2)])
            scn = tok_rot[0][:, 0:TC]
            for j in range(ND):
                wte(v, ev("dve"))
                tick("dve", v.tensor_tensor_scan(
                    scn, aac[:, j:j + 1].broadcast_to([P, TC]),
                    ug[:, j * TC:(j + 1) * TC], carr[:, j:j + 1],
                    op0=ALU.mult, op1=ALU.add))
                wte(v, ev("dve"))
                tick("dve", v.tensor_copy(carr[:, j:j + 1],
                                          scn[:, TC - 1:TC]))
            E[("a_carr", c)] = tick("dve", v.tensor_copy(
                carrh[:, (c + 1) * 16:(c + 2) * 16], carr[:]))

        def a_xload_tile(p, g):
            if g >= NCH * NT:
                return
            sft = g % NT
            if g >= 2:
                wte(p, E[("a_xn", g - 2)])
            E[("a_xl", g)] = tick("ax%d" % (sft % 2), p.dma_start(
                out=x_rot[sft % 2][:],
                in_=x_in[(g // NT) * TC + sft * P:
                         (g // NT) * TC + (sft + 1) * P, :]), 16)

        def a_xload(p, c):
            for g in range(c * NT + 2, c * NT + 6):
                a_xload_tile(p, g)

        def a_ugstore(p, c):
            wte(p, E[("a_ugl", c)])
            E[("a_ugst", c)] = tick("ust",
                                    p.dma_start(out=ugs[c], in_=ug[:]), 16)

        def fg_load(s, c, j, hh):
            q = c * 2 * NI + 2 * j + hh
            slot = (2 * j + hh) % 4
            if q >= 4:
                pq = q - 4
                wte(s, E[("c_gmh", pq // (2 * NI),
                          (pq % (2 * NI)) // 2, pq % 2)])
            wte(s, E[("wout", ND + j)])
            E[("c_fgl", c, j, hh)] = tick(
                "fg%d" % slot,
                s.dma_start(out=fgh[slot][:],
                            in_=wgfs[j][:, hh * 1024:(hh + 1) * 1024]), 16)

        def gu_sync(s, c, stop):
            if c == 0:
                for half in range(4):
                    fg_load(s, c, half // 2, half % 2)
            for half in range(4, 2 * NI):
                fg_load(s, c, half // 2, half % 2)

        def wd_load(s, c, j2, hf):
            jj2 = c * ND + j2
            hb = (j2 * 2 + hf) % 4
            if jj2 >= 2:
                wte(s, E[("c_dg", (jj2 - 2) // ND, (jj2 - 2) % ND)])
            if c == 0 and hb == 3:
                wte(s, E["W_spl0"])
                wte(s, E["W_spl1"])
            sb = mats[3]["segbase"]
            wte(s, E[("wout", sb + j2 * 4 + 2 * hf)])
            wte(s, E[("wout", sb + j2 * 4 + 2 * hf + 1)])
            E[("c_wdl", c, j2, hf)] = tick(
                "wd%d" % hb,
                s.dma_start(out=wd_bufs[hb],
                            in_=wdfs[j2][:, hf * 4096:(hf + 1) * 4096]),
                16)

        def gs_store(s, c, j2):
            wte(s, E[("c_gf", c, j2)])
            E[("c_gs", c, j2)] = tick(
                "gs%d" % (j2 % 2),
                s.dma_start(out=gfs[c][:, j2 * TC:(j2 + 1) * TC],
                            in_=gf_rot[j2 % 2][:]), 16)

        def down_sync(s, c):
            if c + 1 < NCH:
                a_wg_load(s, c + 1, 0)
            if c == 0:
                for j2 in (0, 1):
                    for hf in range(2):
                        wd_load(s, c, j2, hf)
            if c + 1 < NCH:
                a_wg_load(s, c + 1, 1)
            for j2 in range(2, ND):
                gs_store(s, c, j2 - 2)
                for hf in range(2):
                    wd_load(s, c, j2, hf)
                if c + 1 < NCH:
                    a_wg_load(s, c + 1, j2)
                if j2 == 4 and c + 1 < NCH:
                    for half in range(4):
                        fg_load(s, c + 1, half // 2, half % 2)
            gs_store(s, c, ND - 2)
            gs_store(s, c, ND - 1)
            if c + 1 < NCH:
                for j2 in (0, 1):
                    for hf in range(2):
                        wd_load(s, c + 1, j2, hf)

        def up_sync(p, c, stop):
            for j in range(NI):
                w_take_until(p, "pli", 16 + 2 * j + 2, stop)
                w_take_until(p, "plo", 16 + 2 * j + 2, stop)
                if j == 32:
                    a_xload_tile(p, c * NT + 6)
                    a_xload_tile(p, c * NT + 7)
                jj = c * NI + j
                if jj >= 2:
                    wte(p, E[("c_gu", (jj - 2) // NI, (jj - 2) % NI)])
                wte(p, E[("wout", ND + NI + j)])
                E[("c_wul", c, j)] = tick(
                    "wu%d" % (j % 2),
                    p.dma_start(out=wstr_u[j % 2][:], in_=wufs[j]), 16)
                w_take(p, "pli", 1, stop)
                w_take(p, "plo", 1, stop)
            w_take(p, "pli", 2, stop)
            for _ in range(NSEG_ALL):
                w_take(p, "plo", 1, stop)
                w_take(p, "pli", 1, stop)
            w_take(p, "plo", NSEG_ALL, stop)
            a_xload_tile(p, c * NT + 8)
            a_xload_tile(p, c * NT + 9)

        def gu_pe(t, c, stop):
            for j in range(NI):
                w_take(t, "pe", 2, stop)
                gb = (j % 2) * 2
                ub = gb + 1
                if j == 0:
                    wte(t, E[("a_tevl", c)])
                    if c == NCH - 1:
                        wte(t, E["b_rows"])
                if j >= 2:
                    wte(t, E[("c_sg", c, j - 2)])
                else:
                    wte(t, E[("a_sig", c, 12 + gb)])
                for hh in range(2):
                    wte(t, E[("c_fgl", c, j, hh)])
                    for k in range(8 * hh, 8 * hh + 8):
                        ins = t.matmul(
                            psF[gb][:],
                            fgh[(2 * j + hh) % 4][:, (k - 8 * hh) * P:
                                                  (k - 8 * hh + 1) * P],
                            xnT[:, k * TC:(k + 1) * TC],
                            start=(k == 0), stop=(k == ND - 1))
                    E[("c_gmh", c, j, hh)] = tick("pe", ins)
                E[("c_gmm", c, j)] = ("pe", n["pe"])
                wte(t, E[("c_wul", c, j)])
                if j >= 2:
                    wte(t, E[("c_h", c, j - 2)])
                else:
                    wte(t, E[("a_sig", c, 12 + ub)])
                for k in range(ND):
                    ins = t.matmul(psF[ub][:],
                                   wstr_u[j % 2][:, k * P:(k + 1) * P],
                                   xnT[:, k * TC:(k + 1) * TC],
                                   start=(k == 0), stop=(k == ND - 1))
                tick("pe", ins)
                E[("c_umm", c, j)] = ("pe", n["pe"])
                E[("c_gu", c, j)] = ("pe", n["pe"])

        def down_pe(t, c, stop, inserts=None):
            for j2 in range(ND):
                if inserts and j2 in inserts:
                    inserts[j2](t)
                if c + 1 < NCH:
                    if j2 == 0:
                        a_pe_tp(t, c + 1, half=1)
                    else:
                        a_pe_gg(t, c + 1, j2 - 1)
                if c == 0 and j2 == 1:
                    w_take(t, "pe", NSEG_ALL, stop)
                w_take(t, "pe", 4, stop)
                jj2 = c * ND + j2
                db = 4 + j2 % 2
                if j2 == 0:
                    wte(t, E[("c_h", c, NI - 1)])
                if jj2 >= 2:
                    wte(t, E[("c_gf", (jj2 - 2) // ND, (jj2 - 2) % ND)])
                for hf in range(2):
                    wte(t, E[("c_wdl", c, j2, hf)])
                    for kk in range(32):
                        k = hf * 32 + kk
                        ins = t.matmul(
                            psF[db][:],
                            wd_bufs[(j2 * 2 + hf) % 4][:, kk * P:(kk + 1) * P],
                            h[:, k * TC:(k + 1) * TC],
                            start=(k == 0), stop=(k == NI - 1))
                E[("c_dg", c, j2)] = tick("pe", ins)
            if c + 1 < NCH:
                a_pe_gg(t, c + 1, ND - 1)

        def gu_act(a, c, stop):
            for j in range(NI):
                w_take(a, "act", 2, stop)
                if c + 1 < NCH:
                    if j == 24:
                        a_act_sqrt(a, c + 1, 0)
                    elif j == 44:
                        a_act_sqrt(a, c + 1, 1)
                jj = c * NI + j
                wte(a, E[("c_gmm", c, j)])
                if jj >= 2:
                    jp = jj - 2
                    wte(a, E[("c_h", jp // NI, jp % NI)])
                E[("c_sg", c, j)] = tick("act", a.activation(
                    sg_rot[j % 2][:], psF[(j % 2) * 2][:], AF.Silu,
                    scale=gscc[:, j:j + 1]))

        def down_act(a, c):
            for j2 in range(ND):
                if c == 0 and j2 == 1:
                    w_take(a, "act", NSEG_ALL, NSEG_ALL)
                w_take(a, "act", 12 if j2 == 0 else 4, NSEG_ALL)
                jj2 = c * ND + j2
                wte(a, E[("c_dg", c, j2)])
                if jj2 >= 2:
                    wte(a, E[("c_gs", (jj2 - 2) // ND, (jj2 - 2) % ND)])
                E[("c_gf", c, j2)] = tick("act", a.activation(
                    gf_rot[j2 % 2][:], psF[4 + j2 % 2][:], AF.Silu,
                    scale=col(CDSC, j2)))
                if c + 1 < NCH and j2 >= 1:
                    a_act_sig(a, c + 1, j2 - 1)
            if c + 1 < NCH:
                a_act_sig(a, c + 1, ND - 1)

        def gu_dve(v, c, stop):
            for j in range(NI):
                w_take(v, "dve", 2, stop)
                if c + 1 < NCH:
                    if j == 16:
                        a_ln_stats(v, c + 1, 0)
                    elif j == 28:
                        a_ln_norm(v, c + 1, 0)
                    elif j == 36:
                        a_ln_stats(v, c + 1, 1)
                wte(v, E[("c_umm", c, j)])
                wte(v, E[("c_sg", c, j)])
                if j == 0 and c > 0:
                    wte(v, E[("c_dg", c - 1, ND - 1)])
                E[("c_h", c, j)] = tick("dve", v.scalar_tensor_tensor(
                    h[:, j * TC:(j + 1) * TC],
                    psF[(j % 2) * 2 + 1][:], uscc[:, j:j + 1],
                    sg_rot[j % 2][:],
                    op0=ALU.mult, op1=ALU.mult))

        def b_dve1(v):
            for j in range(ND):
                tick("dve", v.reduce_sum(col(CUST, j),
                                         usall[:, j * 8:(j + 1) * 8], axis=AX))
            E["b_ust"] = ev("dve")

        def b_pe1(t):
            wte(t, E[("a_carr", NCH - 1)])
            wte(t, E["b_ust"])
            wte(t, E[("a_sig", NCH - 1, 12)])
            wte(t, E[("a_sig", NCH - 1, 13)])
            ins = t.transpose(psF[0][0:16, 0:P], carrh[:, 8 * 16: 9 * 16], idf[:])
            E["b_ctp"] = tick("pe", ins)
            ins = t.transpose(psF[1][0:16, 0:P], col(CUST), idf[:])
            E["b_utp"] = tick("pe", ins)

        def b_dve2(v):
            wte(v, E["b_ctp"])
            wte(v, E[("wtp", (NSEG_ALL - 1) * 4 + 3)])
            tick("dve", v.tensor_copy(carrT[:], psF[0][0:16, 0:P]))
            wte(v, E["b_utp"])
            tick("dve", v.tensor_copy(ustotT[:], psF[1][0:16, 0:P]))
            E["b_rows"] = ev("dve")

        def b_sync1(s):
            wte(s, E["b_rows"])
            tick("cci", s.dma_start(
                out=ccin_ap[0:1, :].rearrange("o (j p) -> j (o p)", p=P),
                in_=carrT[:]), 16)
            tick("cci", s.dma_start(
                out=ccin_ap[1:2, :].rearrange("o (j p) -> j (o p)", p=P),
                in_=ustotT[:]), 16)
            E["b_ccin"] = ev("cci")

        def b_cc(p):
            wte(p, E["b_ccin"])
            tick("cc", p.collective_compute(
                "AllGather", ALU.bypass,
                replica_groups=[list(range(8))],
                ins=[ccin_ap.opt()],
                outs=[ccout_ap.opt()]), 1)

        def b_sync2(s):
            s.wait_ge(sems["cc"], 1)
            for g in range(4):
                wte(s, E[("wcq", NSEG_ALL - 1, g)])
            tick("ccg", s.dma_start(out=gcol, in_=ccout_ap[:, :]), 16)
            tick("ccg", s.dma_start(out=dlrvc[:], in_=dlrv[:]), 16)
            E["b_g"] = ev("ccg")

        def b_pe2(t):
            wte(t, E["b_g"])
            wte(t, E[("c_sg", NCH - 1, 63)])
            wte(t, E[("c_h", NCH - 1, 63)])
            for j in range(ND):
                ins = t.transpose(psF[2][:, j * 16:(j + 1) * 16],
                                  gcol[:, j * P:(j + 1) * P], idf[32:48, 32:48])
                tick("pe", ins)
            E["b_gtp"] = ev("pe")
            for m in range(4):
                ins = t.matmul(psF[3][:, m * 16:(m + 1) * 16], onesr[:],
                               maskc[0:1, m * 16:(m + 1) * 16],
                               start=True, stop=True)
                tick("pe", ins)
            E["b_mbc"] = ev("pe")

        def b_dve3(v):
            wte(v, E["b_gtp"])
            wte(v, E["b_mbc"])
            mbc = sf[:, 0:64]
            tick("dve", v.tensor_copy(mbc, psF[3][:, 0:64]))
            wte(v, ev("dve"))
            for m in range(4):
                for j in range(ND):
                    tick("dve", v.scalar_tensor_tensor(
                        jk16[:], psF[2][:, j * 16:(j + 1) * 16], 1.0,
                        mbc[:, m * 16:(m + 1) * 16],
                        op0=ALU.mult, op1=ALU.mult,
                        accum_out=qm[:, m * 16 + j: m * 16 + j + 1]))
            tick("dve", v.tensor_copy(pwa[:], aac))
            cur, nxt = pwa, pwb
            for step in range(12):
                wte(v, ev("dve"))
                tick("dve", v.tensor_tensor(nxt[:], cur[:], cur[:], op=ALU.mult))
                if step == 8:
                    tick("dve", v.tensor_copy(a512[:], nxt[:]))
                cur, nxt = nxt, cur
            tick("dve", v.tensor_copy(pw1[:], cur[:]))
            wte(v, ev("dve"))
            tick("dve", v.tensor_tensor(pw2[:], pw1[:], pw1[:], op=ALU.mult))
            wte(v, ev("dve"))
            tick("dve", v.tensor_tensor(jk16[:], pw1[:], qm[:, 16:32], op=ALU.mult))
            wte(v, ev("dve"))
            tick("dve", v.tensor_tensor(sin_t[:], qm[:, 0:16], jk16[:], op=ALU.add))
            wte(v, ev("dve"))
            tick("dve", v.tensor_tensor(jk16[:], pw2[:], qm[:, 32:48], op=ALU.mult))
            wte(v, ev("dve"))
            tick("dve", v.tensor_tensor(sin_t[:], sin_t[:], jk16[:], op=ALU.add))
            tick("dve", v.tensor_scalar(col(CUM), qm[:, 48:64], 1.0 / 16384.0,
                                        None, op0=ALU.mult))
            E["b_umean"] = ev("dve")

        def b_pe3(t):
            wte(t, E["b_umean"])
            for k in range(ND):
                ins = t.matmul(psF[0][0:4, 0:1], dlruc[:, k * 4:(k + 1) * 4],
                               col(CUM, k), start=(k == 0), stop=(k == ND - 1))
            E["b_r"] = tick("pe", ins)

        def b_dve4(v):
            wte(v, E["b_r"])
            E["b_rs"] = tick("dve", v.tensor_copy(rsb[:], psF[0][0:4, 0:1]))

        def b_pe4(t):
            wte(t, E["b_rs"])
            for j in range(ND):
                ins = t.matmul(psF[1][:, j:j + 1],
                               dlrvc[0:4, j * P:(j + 1) * P], rsb[:],
                               start=True, stop=True)
                tick("pe", ins)
            E["b_dlr"] = ev("pe")

        def b_dve5(v):
            wte(v, E["b_dlr"])
            tick("dve", v.tensor_scalar(col(CDLR), psF[1][:, 0:16], 0.25, None,
                                        op0=ALU.mult))
            wte(v, ev("dve"))
            tick("dve", v.tensor_tensor(col(CDEFF), col(CD), col(CDLR),
                                        op=ALU.add))
            tick("dve", v.tensor_copy(cura[:], sin_t[:]))
            cur, nxt = cura, curb
            for c in range(NCH):
                wte(v, ev("dve"))
                tick("dve", v.tensor_tensor(sf[:, c * 16:(c + 1) * 16],
                                            carrh[:, c * 16:(c + 1) * 16],
                                            cur[:], op=ALU.add))
                tick("dve", v.tensor_tensor(nxt[:], cur[:], a512[:], op=ALU.mult))
                cur, nxt = nxt, cur
            E["b_done"] = ev("dve")

        HL = ND * TC // 2

        def dslot(q):
            return convbF[:, q * TC:(q + 1) * TC]

        def d2_sync(s, c2):
            if c2 == 0:
                wte(s, E[("a_ugst", NCH - 1)])
                wte(s, E[("a_carr", NCH - 1)])
            else:
                wte(s, E[("d2_dl", c2 - 1, 7)])
            E[("d2_ugh", c2)] = tick("ugl", s.dma_start(
                out=ug[:, 0:HL], in_=ugs[c2][:, 0:HL]), 16)
            if c2 > 0:
                wte(s, E[("d2_dl", c2 - 1, 15)])
            E[("d2_ugl", c2)] = tick("ugl1", s.dma_start(
                out=ug[:, HL:2 * HL], in_=ugs[c2][:, HL:2 * HL]), 16)
            for g in range(4):
                G = c2 * 4 + g
                if G < 2:
                    wte(s, E[("a_xn", 30 + G)])
                else:
                    pg = G - 2
                    wte(s, E[("d2_dl", pg // 4, (pg % 4) * 4 + 3)])
                wte(s, E[("c_gs", c2, g * 4 + 2)])
                wte(s, E[("c_gs", c2, g * 4 + 3)])
                E[("d2_gfl", c2, g)] = tick(
                    "gfl%d" % g,
                    s.dma_start(out=x_rot[g % 2][:],
                                in_=gfs[c2][:, g * D:(g + 1) * D]), 16)

        def d2_dve(v, c2):
            def oev(gg):
                for sft in range(NT):
                    wte(v, E[("d2_tp", c2, gg, sft)])
                    if c2 == 0 and gg < 2:
                        wte(v, E[("a_tpl", 30 + gg)])
                    if E.get(("d2_ost", c2, gg - 2, sft)):
                        wte(v, E[("d2_ost", c2, gg - 2, sft)])
                    elif c2 > 0 and gg < 2:
                        wte(v, E[("d2_ost", c2 - 1, gg + 2, sft)])
                    E[("d2_oev", c2, gg, sft)] = tick("dve", v.tensor_copy(
                        tok_rot[gg % 2][:, sft * TC:(sft + 1) * TC],
                        psF[sft][:]))

            for gg in range(4):
                G = c2 * 4 + gg
                wte(v, E[("d2_gfl", c2, gg)])
                if G >= 1:
                    wte(v, E[("d2_tp", (G - 1) // 4, (G - 1) % 4, NT - 1)])
                for q in range(4):
                    j2 = gg * 4 + q
                    if j2 == 0:
                        wte(v, E[("d2_ugh", c2)])
                        if c2 == 0:
                            wte(v, E["b_done"])
                            wte(v, E[("a_carr", NCH - 1)])
                            wte(v, E[("wtp", (NSEG_ALL - 1) * 4 + 3)])
                            wte(v, E[("c_h", NCH - 1, 62)])
                            wte(v, E[("c_h", NCH - 1, 63)])
                    elif j2 == 8:
                        wte(v, E[("d2_ugl", c2)])
                    s2 = j2 % 2
                    wte(v, ev("dve"))
                    tick("dve", v.tensor_tensor_scan(
                        fghF[s2][:],
                        aac[:, j2:j2 + 1].broadcast_to([P, TC]),
                        ug[:, j2 * TC:(j2 + 1) * TC],
                        sf[:, c2 * 16 + j2: c2 * 16 + j2 + 1],
                        op0=ALU.mult, op1=ALU.add))
                    wte(v, ev("dve"))
                    tick("dve", v.tensor_scalar(
                        y_rot[s2][:], fghF[s2][:], col(CCB, j2),
                        None, op0=ALU.mult))
                    E[("d2_y", c2, j2)] = tick("dve", v.scalar_tensor_tensor(
                        fghF[s2][:], ug[:, j2 * TC:(j2 + 1) * TC],
                        col(CDEFF, j2), y_rot[s2][:],
                        op0=ALU.mult, op1=ALU.add))
                    E[("d2_dl", c2, j2)] = tick("dve", v.tensor_tensor(
                        dslot(q), fghF[s2][:],
                        x_rot[gg % 2][:, q * TC:(q + 1) * TC], op=ALU.mult))
                if gg >= 1:
                    oev(gg - 1)
            oev(3)

        def d2_pe(t, c2):
            for gg in range(4):
                wte(t, E[("d2_dl", c2, gg * 4 + 3)])
                for sft in range(NT):
                    G = c2 * 4 + gg
                    if G >= 1:
                        wte(t, E[("d2_oev", (G - 1) // 4, (G - 1) % 4, sft)])
                    for q in range(4):
                        ins = t.transpose(
                            psF[sft][:, q * P:(q + 1) * P],
                            dslot(q)[:, sft * P:(sft + 1) * P],
                            idf[:])
                    E[("d2_tp", c2, gg, sft)] = tick("pe", ins)
            E[("d2_tpl", c2)] = ev("pe")

        def d2_spl(p, c2):
            if c2 == 0:
                wte(p, E["xpr"])
            for gg in range(4):
                for sft in range(NT):
                    wte(p, E[("d2_oev", c2, gg, sft)])
                    E[("d2_ost", c2, gg, sft)] = tick(
                        "os%d" % ((gg % 2) * 4 + sft), p.dma_start(
                            out=out_d[c2 * TC + sft * P:
                                      c2 * TC + (sft + 1) * P,
                                      gg * TC:(gg + 1) * TC],
                            in_=tok_rot[gg % 2][:, sft * TC:(sft + 1) * TC],
                            accum_op=ALU.add), 16)

        sect("sync", lambda s: a_sync(s, 0))
        sect("gpsimd", lambda p: a_xload(p, 0))

        def a0_act(a):
            a_act_sqrt(a, 0, 0)
            a_act_sqrt(a, 0, 1)
            for j in range(ND):
                a_act_sig(a, 0, j)
        sect("scalar", a0_act)

        def a0_dve(v):
            a_ln_pair(v, 0, 0)
            a_evacs_half(v, 0, 0)
            a_ln_pair(v, 0, 1)
            a_evacs_half(v, 0, 1)
            a_ug_scan(v, 0)
        sect("vector", a0_dve)
        sect("tensor", lambda t: a_pe(t, 0))

        for c in range(NCH):
            stop = W_STOP[c]
            last = (c == NCH - 1)

            sect("sync", lambda s, c=c, stop=stop: gu_sync(s, c, stop))
            if c > 0:
                sect("gpsimd", lambda p, c=c: a_ugstore(p, c))
            sect("gpsimd", lambda p, c=c, stop=stop: up_sync(p, c, stop))
            if c == 0:
                sect("gpsimd", lambda p: a_ugstore(p, 0))
            if last:
                sect("vector", b_dve1)
                sect("tensor", b_pe1)
                sect("vector", b_dve2)
            sect("scalar", lambda a, c=c, stop=stop: gu_act(a, c, stop))
            sect("vector", lambda v, c=c, stop=stop: gu_dve(v, c, stop))
            sect("tensor", lambda t, c=c, stop=stop: gu_pe(t, c, stop))

            if not last:
                sect("tensor", lambda t, c=c: a_pe_tp(t, c + 1, half=0))
                sect("vector", lambda v, c=c: (a_evacs_half(v, c + 1, 0),
                                               a_ln_norm(v, c + 1, 1),
                                               a_evacs_half(v, c + 1, 1),
                                               a_ug_scan(v, c + 1)))
                sect("sync", lambda s, c=c: down_sync(s, c))
                sect("tensor", lambda t, c=c, stop=stop: down_pe(t, c, stop))
                sect("scalar", lambda a, c=c: down_act(a, c))
            else:
                sect("gpsimd", b_sync1)
                sect("gpsimd", b_cc)
                sect("gpsimd", b_sync2)
                sect("vector", b_dve3)
                sect("vector", b_dve4)
                sect("vector", b_dve5)
                sect("sync", lambda s, c=c: down_sync(s, c))
                sect("tensor", lambda t, c=c, stop=stop: down_pe(
                    t, c, stop, inserts={6: b_pe2, 10: b_pe3, 13: b_pe4}))
                sect("scalar", lambda a, c=c: down_act(a, c))

        for c2 in range(NCH):
            sect("sync", lambda s, c2=c2: d2_sync(s, c2))
            sect("tensor", lambda t, c2=c2: d2_pe(t, c2))
            sect("vector", lambda v, c2=c2: d2_dve(v, c2))
            sect("gpsimd", lambda p, c2=c2: d2_spl(p, c2))

        def tail(p):
            for k in ("os0", "os1", "os2", "os3", "os4", "os5", "os6", "os7",
                      "ust", "wo0", "wo1", "xpr", "gs0", "gs1"):
                p.wait_ge(sems[k], n[k])
        sect("gpsimd", tail)

        stub = _StubEngine()
        for k in ("dve", "act", "pe", "pli", "plo"):
            wcur[k] = 0
        for engname, fn in sections:
            fn(stub)
        pass1 = dict(n)
        pass1_wcur = dict(wcur)
        assert all(v == NSEG_ALL for v in pass1_wcur.values()), pass1_wcur
        for k in n:
            n[k] = 0
        for k in wcur:
            wcur[k] = 0
        E.pass2 = True
        for engname, fn in sections:
            getattr(block, engname)(fn)
        assert n == pass1, f"two-pass mismatch: {n} vs {pass1}"
        assert wcur == pass1_wcur, f"wcur mismatch: {wcur} vs {pass1_wcur}"

    return nc


_NC_CACHE = None


def kernel(**inputs):
    global _NC_CACHE
    x = np.ascontiguousarray(inputs["L0_attn_u"])
    B, T, Dm = x.shape
    assert (B, T, Dm) == (2, 16384, 2048)

    def row(v):
        return np.ascontiguousarray(np.asarray(v, dtype=np.float32).reshape(1, -1))

    shared = dict(
        av=row(inputs["L0_attn_a"]),
        bv=row(inputs["L0_attn_b"]),
        cv=row(inputs["L0_attn_c"]),
        dv=row(inputs["L0_attn_d"]),
        gbv=row(inputs["L0_attn_gate_b"]),
        dscv=row(inputs["L0_ffn_down_scale"]),
        gscv=row(inputs["L0_ffn_gate_scale"]),
        uscv=row(inputs["L0_ffn_up_scale"]),
        gate_w=np.ascontiguousarray(inputs["L0_attn_gate_w"], dtype=np.float32),
        gq=np.ascontiguousarray(inputs["L0_ffn_gate_w"], dtype=np.int32),
        uq=np.ascontiguousarray(inputs["L0_ffn_up_w"], dtype=np.int32),
        dq=np.ascontiguousarray(inputs["L0_ffn_down_w"], dtype=np.int32),
        dlru=np.ascontiguousarray(inputs["L0_attn_d_lr_u"], dtype=np.float32),
        dlrv=np.ascontiguousarray(inputs["L0_attn_d_lr_v"], dtype=np.float32),
        identf=np.eye(P, dtype=np.float32),
        identb=np.eye(P, dtype=np.float32).astype(ml_dtypes.bfloat16),
    )

    in_maps = []
    for core in range(8):
        b, q = core // 4, core % 4
        msk = np.zeros((1, 64), dtype=np.float32)
        for j in range(4):
            m = q - 1 - j
            if 0 <= m <= 2:
                msk[0, m * 16 + 2 * (4 * b + j)] = 1.0
            msk[0, 48 + 2 * (4 * b + j) + 1] = 1.0
        im = dict(shared)
        im["x"] = np.ascontiguousarray(x[b, q * TCORE:(q + 1) * TCORE, :])
        im["masks"] = msk
        in_maps.append(im)

    if _NC_CACHE is None:
        _NC_CACHE = build_nc()
    res = run_bass_kernel_spmd(_NC_CACHE, in_maps, core_ids=list(range(8)))

    out = np.empty((B, T, Dm), dtype=np.float32)
    for core in range(8):
        b, q = core // 4, core % 4
        out[b, q * TCORE:(q + 1) * TCORE, :] = res.results[core]["out"]
    return out


# version 19 (final)
# speedup vs baseline: 1.0522x; 1.0522x over previous
import numpy as np
import ml_dtypes
import concourse.bass as bass
import concourse.mybir as mybir
from concourse.bass_utils import run_bass_kernel_spmd
from contextlib import ExitStack

F32 = mybir.dt.float32
BF16 = mybir.dt.bfloat16
I32 = mybir.dt.int32
AF = mybir.ActivationFunctionType
ALU = mybir.AluOpType
AX = mybir.AxisListType.X

P = 128
D = 2048
ND = 16
II = 8192
NI = 64
TCORE = 4096
TC = 512
NCH = 8
NT = 4
EPS = 1e-5


class _StubInstr:
    def then_inc(self, *a, **k):
        return self


class _StubEngine:
    def __getattr__(self, name):
        def f(*a, **k):
            return _StubInstr()
        return f


class _EvDict(dict):
    pass2 = False

    def __missing__(self, key):
        if self.pass2:
            raise KeyError(key)
        return 0


def build_nc():
    nc = bass.Bass("TRN2", target_bir_lowering=False)

    x_in = nc.dram_tensor("x", [TCORE, D], F32, kind="ExternalInput").ap()
    av = nc.dram_tensor("av", [1, D], F32, kind="ExternalInput").ap()
    bv = nc.dram_tensor("bv", [1, D], F32, kind="ExternalInput").ap()
    cv = nc.dram_tensor("cv", [1, D], F32, kind="ExternalInput").ap()
    dv = nc.dram_tensor("dv", [1, D], F32, kind="ExternalInput").ap()
    gbv = nc.dram_tensor("gbv", [1, D], F32, kind="ExternalInput").ap()
    dscv = nc.dram_tensor("dscv", [1, D], F32, kind="ExternalInput").ap()
    gscv = nc.dram_tensor("gscv", [1, II], F32, kind="ExternalInput").ap()
    uscv = nc.dram_tensor("uscv", [1, II], F32, kind="ExternalInput").ap()
    gate_w = nc.dram_tensor("gate_w", [D, D], F32, kind="ExternalInput").ap()
    gq = nc.dram_tensor("gq", [II, D], I32, kind="ExternalInput").ap()
    uq = nc.dram_tensor("uq", [II, D], I32, kind="ExternalInput").ap()
    dq = nc.dram_tensor("dq", [D, II], I32, kind="ExternalInput").ap()
    dlru = nc.dram_tensor("dlru", [D, 4], F32, kind="ExternalInput").ap()
    dlrv = nc.dram_tensor("dlrv", [4, D], F32, kind="ExternalInput").ap()
    identf = nc.dram_tensor("identf", [P, P], F32, kind="ExternalInput").ap()
    identb = nc.dram_tensor("identb", [P, P], BF16, kind="ExternalInput").ap()
    masks = nc.dram_tensor("masks", [1, 64], F32, kind="ExternalInput").ap()

    out_d = nc.dram_tensor("out", [TCORE, D], F32, kind="ExternalOutput").ap()

    wgs = nc.dram_tensor("wgs", [ND, P, D], BF16).ap()
    wgfs = nc.dram_tensor("wgfs", [NI, P, D], BF16).ap()
    wufs = nc.dram_tensor("wufs", [NI, P, D], BF16).ap()
    wdfs = nc.dram_tensor("wdfs", [ND, P, II], BF16).ap()
    ugs = nc.dram_tensor("ugs", [NCH, P, ND * TC], BF16).ap()
    gfs = nc.dram_tensor("gfs", [NCH, P, ND * TC], F32).ap()
    ccin_ap = nc.dram_tensor("ccin", [2, D], F32).ap()
    ccout_ap = nc.dram_tensor("ccout", [16, D], F32).ap()

    with ExitStack() as ctx:
        e = ctx.enter_context

        h = e(nc.sbuf_tensor("h", [P, NI * TC], BF16))
        ug = e(nc.sbuf_tensor("ug", [P, ND * TC], BF16))
        xnT = e(nc.sbuf_tensor("xnT", [P, ND * TC], BF16))
        x_rot = [e(nc.sbuf_tensor(f"xr{i}", [P, D], F32)) for i in range(2)]
        tok_rot = [e(nc.sbuf_tensor(f"tr{i}", [P, D], F32)) for i in range(2)]
        wstr_g = [e(nc.sbuf_tensor(f"wg{i}", [P, 1024], BF16)) for i in range(2)]
        wstr_u = [e(nc.sbuf_tensor(f"wu{i}", [P, D], BF16)) for i in range(2)]
        wstr_d = [e(nc.sbuf_tensor(f"wd{i}", [P, 4096], BF16)) for i in range(2)]
        wdc = e(nc.sbuf_tensor("wdc", [P, 4096], BF16))
        wes = e(nc.sbuf_tensor("wes", [P, 4096], BF16))
        fgh = [e(nc.sbuf_tensor(f"fh{i}", [P, 1024], BF16)) for i in range(4)]
        sg_rot = [e(nc.sbuf_tensor(f"sg{i}", [P, TC], F32)) for i in range(2)]
        gf_rot = [e(nc.sbuf_tensor(f"gf{i}", [P, TC], F32)) for i in range(2)]
        y_rot = sg_rot
        fghF = [f.bitcast(F32) for f in fgh]

        idf = e(nc.sbuf_tensor("idf", [P, P], F32))
        idb = e(nc.sbuf_tensor("idb", [P, P], BF16))
        cols = e(nc.sbuf_tensor("cols", [P, 16 * 11], F32))
        gscc = e(nc.sbuf_tensor("gscc", [P, NI], F32))
        uscc = e(nc.sbuf_tensor("uscc", [P, NI], F32))
        dlruc = e(nc.sbuf_tensor("dlruc", [P, 64], F32))
        bigB = e(nc.sbuf_tensor("bigB", [P, 3 * 512], F32))
        bigBI = bigB.bitcast(I32)
        maskc = e(nc.sbuf_tensor("maskc", [1, 64], F32))
        onesr = e(nc.sbuf_tensor("onesr", [1, P], F32))
        qm = e(nc.sbuf_tensor("qm", [P, 64], F32))
        jk16 = e(nc.sbuf_tensor("jk16", [P, 16], F32))
        pwa = e(nc.sbuf_tensor("pwa", [P, 16], F32))
        pwb = e(nc.sbuf_tensor("pwb", [P, 16], F32))
        a512 = e(nc.sbuf_tensor("a512", [P, 16], F32))
        pw1 = e(nc.sbuf_tensor("pw1", [P, 16], F32))
        pw2 = e(nc.sbuf_tensor("pw2", [P, 16], F32))
        sin_t = e(nc.sbuf_tensor("sin_t", [P, 16], F32))
        cura = e(nc.sbuf_tensor("cura", [P, 16], F32))
        curb = e(nc.sbuf_tensor("curb", [P, 16], F32))
        rsb = e(nc.sbuf_tensor("rsb", [4, 1], F32))
        carr = e(nc.sbuf_tensor("carr", [P, 16], F32))
        carrh = e(nc.sbuf_tensor("carrh", [P, 16 * 9], F32))
        sf = e(nc.sbuf_tensor("sf", [P, 16 * 8], F32))
        convb = e(nc.sbuf_tensor("convb", [P, 2 * D], BF16))
        convbF = convb.bitcast(F32)
        vrow = convbF[0:1, 0:D]
        usall = e(nc.sbuf_tensor("usall", [P, 128], F32))
        lnst = e(nc.sbuf_tensor("lnst", [P, 16], F32))
        lnbn = e(nc.sbuf_tensor("lnbn", [P, 2 * 24], F32))
        epst = e(nc.sbuf_tensor("epst", [P, 1], F32))
        m128t = e(nc.sbuf_tensor("m128t", [P, 1], F32))
        carrT = convbF[0:16, 0:P]
        ustotT = convbF[0:16, P:2 * P]
        gcol = convbF[32:48, :]
        dlrvc = convbF[0:4, :]

        wd_bufs = [wstr_d[0][:, :], wstr_d[1][:, :], wdc[:, :], wes[:, :]]

        psF = [e(nc.psum_tensor(f"psF{i}", [P, TC], F32)) for i in range(6)]
        psT = [e(nc.psum_tensor(f"psT{i}", [P, 2 * TC], BF16)) for i in range(2)]

        sems = {k: e(nc.semaphore(f"s_{k}")) for k in
                ("pe", "dve", "act", "cc",
                 "cst", "vrw", "ax0", "ax1",
                 "wi0", "wi1", "wi2", "wi3", "wi4", "wi5", "wi6", "wi7",
                 "wi8", "wi9", "wi10", "wi11", "wi12", "wi13", "wi14",
                 "wg0", "wg1", "wu0", "wu1",
                 "wd0", "wd1", "wd2", "wd3",
                 "fg0", "fg1", "fg2", "fg3",
                 "gs0", "gs1", "gfl0", "gfl1", "gfl2", "gfl3",
                 "ugl", "ugl1", "cci", "ccg",
                 "ust", "wo0", "wo1", "xpr",
                 "os0", "os1", "os2", "os3", "os4", "os5", "os6", "os7")}
        block = e(nc.Block())
        n = {k: 0 for k in sems}
        E = _EvDict()

        def tick(key, instr, k=1):
            instr.then_inc(sems[key], k)
            n[key] += k
            return (key, n[key])

        def ev(key):
            return (key, n[key])

        def wte(eng, evt):
            if evt:
                eng.wait_ge(sems[evt[0]], evt[1])

        CA, CB, CC, CD, CGB, CDSC, CCB, CDLR, CDEFF, CUM, CUST = range(11)

        def col(idx, j=None):
            if j is None:
                return cols[:, idx * 16:(idx + 1) * 16]
            return cols[:, idx * 16 + j: idx * 16 + j + 1]

        sections = []

        class _SelfSync:

            def __init__(self, eng, key):
                self._eng = eng
                self._key = key
                self._last = -1

            def wait_ge(self, *a, **k):
                return self._eng.wait_ge(*a, **k)

            def __getattr__(self, name):
                real = getattr(self._eng, name)

                def f(*a, **k):
                    if n[self._key] > 0 and n[self._key] != self._last:
                        self._eng.wait_ge(sems[self._key], n[self._key])
                        self._last = n[self._key]
                    return real(*a, **k)
                return f

        _selfsync_key = {"vector": "dve", "scalar": "act"}

        def sect(engname, fn):
            if engname in _selfsync_key:
                key = _selfsync_key[engname]
                orig = fn

                def wrapped(eng, _orig=orig, _key=key):
                    return _orig(_SelfSync(eng, _key))
                sections.append((engname, wrapped))
            else:
                sections.append((engname, fn))

        NVEC = 14

        def pre_sync(s):
            tick("cst", s.dma_start(out=idf[:], in_=identf[:]), 16)
            tick("cst", s.dma_start(out=idb[:], in_=identb[:]), 16)
            tick("cst", s.dma_start(out=maskc[:], in_=masks[:]), 16)
            tick("cst", s.dma_start(
                out=dlruc.ap().rearrange("p (j r) -> p j r", j=ND),
                in_=dlru.rearrange("(j p) r -> p j r", p=P)), 16)
            E["pre_const"] = ev("cst")
            E["xpr"] = tick("xpr", s.dma_start(out=out_d[:, :], in_=x_in[:, :]), 16)
            for i in range(NVEC):
                if i >= 1:
                    wte(s, E[("vtp", i - 1)])
                if i < 6:
                    src = [av, bv, cv, dv, gbv, dscv][i][:]
                elif i < 10:
                    src = gscv[0:1, (i - 6) * D:(i - 5) * D]
                else:
                    src = uscv[0:1, (i - 10) * D:(i - 9) * D]
                E[("vrow", i)] = tick("vrw", s.dma_start(out=vrow[:], in_=src), 16)
        sect("sync", pre_sync)

        def pre_pe(t):
            wte(t, E["pre_const"])
            for i in range(NVEC):
                wte(t, E[("vrow", i)])
                if i >= 2:
                    wte(t, E[("vev", i - 2)])
                for k in range(16):
                    ins = t.transpose(psF[i % 2][:, k:k + 1],
                                      vrow[0:1, k * P:(k + 1) * P],
                                      idf[0:1, 0:1])
                E[("vtp", i)] = tick("pe", ins)
        sect("tensor", pre_pe)

        def pre_dve(v):
            for i in range(NVEC):
                wte(v, E[("vtp", i)])
                if i < 6:
                    dst = col([CA, CB, CC, CD, CGB, CDSC][i])
                elif i < 10:
                    dst = gscc[:, (i - 6) * 16:(i - 5) * 16]
                else:
                    dst = uscc[:, (i - 10) * 16:(i - 9) * 16]
                E[("vev", i)] = tick("dve", v.tensor_copy(dst, psF[i % 2][:, 0:16]))
            tick("dve", v.memset(carr[:], 0.0))
            tick("dve", v.memset(carrh[:, 0:16], 0.0))
            tick("dve", v.memset(onesr[:], 1.0))
            tick("dve", v.memset(epst[:], EPS))
            tick("dve", v.memset(m128t[:], -128.0))
            E["pre_dve"] = tick("dve", v.tensor_tensor(col(CCB), col(CB), col(CC),
                                                       op=ALU.mult))
        sect("vector", pre_dve)

        def pre_act(a):
            wte(a, E[("vev", 0)])
            E["tanh1"] = tick("act", a.activation(lnst[:, 0:16], col(CA), AF.Tanh))
        sect("scalar", pre_act)

        def pre_dve2(v):
            wte(v, E["tanh1"])
            E["aacol"] = tick("dve", v.tensor_copy(col(CA), lnst[:, 0:16]))
        sect("vector", pre_dve2)

        aac = col(CA)

        mats = [
            dict(src=gate_w, scr=wgs, tiles=ND, wid=D, int_=False),
            dict(src=gq, scr=wgfs, tiles=NI, wid=D, int_=True),
            dict(src=uq, scr=wufs, tiles=NI, wid=D, int_=True),
            dict(src=dq, scr=wdfs, tiles=ND, wid=II, int_=True),
        ]
        _sb = 0
        for mi, m in enumerate(mats):
            m["segbase"] = _sb
            m["nseg"] = m["wid"] // D
            _sb += m["tiles"] * m["nseg"]
        NSEG_ALL = _sb

        worder = list(range(16))
        for j in range(NI):
            worder.append(16 + j)
            worder.append(16 + NI + j)
        worder.extend(range(16 + 2 * NI, NSEG_ALL))
        assert len(worder) == NSEG_ALL

        def seg_info(si):
            for m in mats:
                if m["segbase"] <= si < m["segbase"] + m["tiles"] * m["nseg"]:
                    rel = si - m["segbase"]
                    return m, rel // m["nseg"], rel % m["nseg"]
            raise AssertionError(si)

        NWS = 15

        def _wslots(view):
            slots = [bigB.bitcast(view) if view != F32 else bigB]
            out = []
            for i in range(3):
                out.append((slots[0], i))
            for t in (wstr_d[0], wstr_d[1], wdc):
                tv = t.bitcast(view)
                for i in range(4):
                    out.append((tv, i))
            return out

        WSLOT_I = _wslots(I32)
        WSLOT_F = _wslots(F32)

        def wstage(qi, int_):
            t, i = (WSLOT_I if int_ else WSLOT_F)[qi % NWS]
            return t[:, i * 512:(i + 1) * 512]

        def w_win(s, oi):
            si = worder[oi]
            m, j, seg = seg_info(si)
            for g in range(4):
                qi = oi * 4 + g
                if qi >= NWS:
                    wte(s, E[("wcq", (qi - NWS) // 4, (qi - NWS) % 4)])
                E[("win", oi, g)] = tick(
                    "wi%d" % (qi % NWS),
                    s.dma_start(out=wstage(qi, m["int_"]),
                                in_=m["src"][j * P:(j + 1) * P,
                                             seg * D + g * 512:
                                             seg * D + (g + 1) * 512]), 16)

        def w_conv(eng, key, oi):
            si = worder[oi]
            m, j, seg = seg_info(si)
            for g in range(4):
                qi = oi * 4 + g
                wte(eng, E[("win", oi, g)])
                if oi >= 2:
                    wte(eng, E[("wtp", (oi - 2) * 4 + g)])
                if oi == 0 and g == 0:
                    wte(eng, E["pre_dve"])
                srcap = wstage(qi, m["int_"])
                dst = convb[:, (oi % 2) * D + g * 512:
                            (oi % 2) * D + (g + 1) * 512]
                if m["int_"]:
                    if key == "dve":
                        ins = eng.tensor_scalar(dst, srcap, -128.0, None,
                                                op0=ALU.add)
                    else:
                        ins = eng.activation(dst, srcap, AF.Prelu,
                                             bias=m128t[:, 0:1], alpha=1.0)
                else:
                    ins = eng.tensor_copy(dst, srcap)
                E[("wcq", oi, g)] = tick(key, ins)

        def w_tp(t, oi):
            for g in range(4):
                gi = oi * 4 + g
                wte(t, E[("wcq", oi, g)])
                if gi >= 2:
                    wte(t, E[("wev", gi - 2)])
                for q in range(4):
                    kb = g * 4 + q
                    ins = t.transpose(
                        psT[gi % 2][:, q * P:(q + 1) * P],
                        convb[:, (oi % 2) * D + kb * P:
                              (oi % 2) * D + (kb + 1) * P],
                        idb[:])
                E[("wtp", gi)] = tick("pe", ins)

        def w_ev(eng, key, oi, g):
            gi = oi * 4 + g
            wte(eng, E[("wtp", gi)])
            if oi >= 2:
                wte(eng, E[("woutO", oi - 2)])
            pb = wes[:, (oi % 2) * D: (oi % 2) * D + D]
            if key == "dve":
                ins = eng.tensor_copy(pb[:, g * TC:(g + 1) * TC],
                                      psT[gi % 2][:, 0:TC])
            else:
                ins = eng.activation(pb[:, g * TC:(g + 1) * TC],
                                     psT[gi % 2][:, 0:TC], AF.Copy)
            E[("wev", gi)] = tick(key, ins)

        def w_out(p, oi):
            si = worder[oi]
            m, j, seg = seg_info(si)
            for g in range(4):
                wte(p, E[("wev", oi * 4 + g)])
            pb = wes[:, (oi % 2) * D: (oi % 2) * D + D]
            evt = tick("wo%d" % (oi % 2),
                       p.dma_start(out=m["scr"][j][:, seg * D:(seg + 1) * D],
                                   in_=pb), 16)
            E[("wout", si)] = evt
            E[("woutO", oi)] = evt
            if oi == NSEG_ALL - 1:
                E["W_spl0"] = ev("wo0")
                E["W_spl1"] = ev("wo1")

        def _w_dve(v, oi):
            if oi < ND or oi % 2 == 0:
                w_conv(v, "dve", oi)
            for g in (0, 2):
                w_ev(v, "dve", oi, g)

        def _w_act(a, oi):
            if oi >= ND and oi % 2 == 1:
                w_conv(a, "act", oi)
            for g in (1, 3):
                w_ev(a, "act", oi, g)

        _w_emit = {"dve": _w_dve, "act": _w_act, "pe": w_tp,
                   "pli": w_win, "plo": w_out}
        wcur = {}

        def w_take(eng, which, k, stop):
            end = min(wcur[which] + k, stop)
            while wcur[which] < end:
                _w_emit[which](eng, wcur[which])
                wcur[which] += 1

        def w_take_until(eng, which, until, stop):
            end = min(until, stop)
            while wcur[which] < end:
                _w_emit[which](eng, wcur[which])
                wcur[which] += 1

        W_STOP = [NSEG_ALL] * NCH

        sect("vector", lambda v: w_take(v, "dve", 16, 16))
        sect("tensor", lambda t: w_take(t, "pe", 16, 16))
        sect("scalar", lambda a: w_take(a, "act", 16, 16))

        def pre_xload(p):
            for g in (0, 1):
                E[("a_xl", g)] = tick("ax%d" % (g % 2), p.dma_start(
                    out=x_rot[g % 2][:],
                    in_=x_in[g * P:(g + 1) * P, :]), 16)
        sect("gpsimd", pre_xload)
        def w_prologue_pl(p):
            for i in range(16):
                w_take(p, "pli", 1, 16)
                if i >= 2:
                    w_take(p, "plo", 1, 14)
            w_take(p, "plo", 2, 16)
        sect("gpsimd", w_prologue_pl)

        def a_wg_load(s, c, j):
            for hh in range(2):
                jj = c * ND + j
                if jj >= 1:
                    wte(s, E[("a_ggh", (jj - 1) // ND, (jj - 1) % ND, hh)])
                if c == 0:
                    wte(s, E[("wout", j)])
                E[("a_wg", c, j, hh)] = tick(
                    "wg%d" % hh,
                    s.dma_start(out=wstr_g[hh][:],
                                in_=wgs[j][:, hh * 1024:(hh + 1) * 1024]),
                    16)

        def a_sync(s, c):
            for j in range(ND):
                a_wg_load(s, c, j)

        def a_act_sqrt(a, c, half):
            g1 = c * NT + 2 * half + 1
            wte(a, E[("a_ag", g1)])
            E[("a_sd", c, half)] = tick("act", a.activation(
                lnst[:, 8 + 2 * half: 10 + 2 * half],
                lnst[:, 4 * half + 1: 4 * half + 4: 2], AF.Sqrt,
                bias=epst[:, 0:1]))

        def sgslot(j):
            return tok_rot[1][:, (j % 4) * TC:((j % 4) + 1) * TC]

        def a_act_sig(a, c, j):
            wte(a, E[("a_gg", c, j)])
            if j < 4:
                wte(a, E[("a_tpl", c * NT + 3)])
                if c > 0:
                    wte(a, E[("a_ug", c - 1, 12 + j)])
            else:
                wte(a, E[("a_ug", c, j - 4)])
            E[("a_sig", c, j)] = tick("act", a.activation(
                sgslot(j), psF[j % 4][:], AF.Sigmoid,
                bias=col(CGB, j)))

        def a_pe_tp(t, c, half=None):
            rng = range(NT) if half is None else range(2 * half, 2 * half + 2)
            for sft in rng:
                if c >= 1:
                    w_take(t, "pe", 2, NSEG_ALL)
                g = c * NT + sft
                wte(t, E[("a_xn", g)])
                for grp in range(4):
                    if sft == 0 and c > 0:
                        if grp == 0:
                            wte(t, E[("c_sg", c - 1, 62)])
                        elif grp == 1:
                            wte(t, E[("c_h", c - 1, 62)])
                        elif grp == 2:
                            wte(t, E[("c_sg", c - 1, 63)])
                        else:
                            wte(t, E[("c_h", c - 1, 63)])
                    elif sft > 0:
                        wte(t, E[("a_tev", g - 1, grp)])
                    for q in range(4):
                        kb = grp * 4 + q
                        ins = t.transpose(
                            psF[grp][:, q * P:(q + 1) * P],
                            tok_rot[sft % 2][:, kb * P:(kb + 1) * P],
                            idf[:])
                    E[("a_tp", g, grp)] = tick("pe", ins)
                E[("a_tpl", g)] = ev("pe")

        def a_pe_gg(t, c, j):
            if j == 0:
                wte(t, E[("a_tevl", c)])
            if j < 4:
                wte(t, E[("a_tev", c * NT + 3, j % 4)])
            else:
                wte(t, E[("a_sig", c, j - 4)])
            for hh in range(2):
                wte(t, E[("a_wg", c, j, hh)])
                for k in range(8 * hh, 8 * hh + 8):
                    ins = t.matmul(
                        psF[j % 4][:],
                        wstr_g[hh][:, (k - 8 * hh) * P:(k - 8 * hh + 1) * P],
                        xnT[:, k * TC:(k + 1) * TC],
                        start=(k == 0), stop=(k == ND - 1))
                E[("a_ggh", c, j, hh)] = tick("pe", ins)
            E[("a_gg", c, j)] = ("pe", n["pe"])

        def a_pe(t, c):
            a_pe_tp(t, c)
            for j in range(ND):
                a_pe_gg(t, c, j)

        def a_ln_stats(v, c, half):
            for i2 in range(2):
                sft = 2 * half + i2
                g = c * NT + sft
                wte(v, E[("a_xl", g)])
                for qq in range(4):
                    tick("dve", v.bn_stats(
                        lnbn[:, sft % 2 * 24 + qq * 6:
                             sft % 2 * 24 + (qq + 1) * 6],
                        x_rot[sft % 2][:, qq * TC:(qq + 1) * TC]))
                E[("a_ag", g)] = tick("dve", v.bn_aggr(
                    lnst[:, 4 * half + 2 * i2: 4 * half + 2 * i2 + 2],
                    lnbn[:, sft % 2 * 24: sft % 2 * 24 + 24]))

        def a_ln_norm(v, c, half):
            wte(v, E[("a_sd", c, half)])
            tick("dve", v.reciprocal(lnst[:, 8 + 2 * half: 10 + 2 * half],
                                     lnst[:, 8 + 2 * half: 10 + 2 * half]))
            for i2 in range(2):
                sft = 2 * half + i2
                g = c * NT + sft
                if g >= 2:
                    wte(v, E[("a_tpl", g - 2)])
                E[("a_xn", g)] = tick("dve", v.tensor_scalar(
                    tok_rot[sft % 2][:], x_rot[sft % 2][:],
                    lnst[:, 4 * half + 2 * i2: 4 * half + 2 * i2 + 1],
                    lnst[:, 8 + sft: 8 + sft + 1],
                    op0=ALU.subtract, op1=ALU.mult))

        def a_ln_pair(v, c, half):
            a_ln_stats(v, c, half)
            a_ln_norm(v, c, half)

        def a_evacs_half(v, c, half):
            for sft in range(2 * half, 2 * half + 2):
                if c >= 1:
                    w_take(v, "dve", 2, NSEG_ALL)
                g = c * NT + sft
                for grp in range(4):
                    wte(v, E[("a_tp", g, grp)])
                    if c > 0 and sft == 0 and grp == 0:
                        wte(v, E[("a_gg", c - 1, ND - 1)])
                        wte(v, E[("c_gu", c - 1, NI - 1)])
                    dst = xnT.ap().rearrange("p (k t) -> p k t", k=ND)[
                        :, grp * 4:(grp + 1) * 4, sft * P:(sft + 1) * P]
                    src = psF[grp].ap().rearrange(
                        "p (q u) -> p q u", q=4)
                    E[("a_tev", g, grp)] = tick("dve", v.tensor_copy(dst, src))
            if half == 1:
                E[("a_tevl", c)] = ev("dve")

        def a_ug_scan(v, c):
            if c == 1:
                w_take(v, "dve", NSEG_ALL, NSEG_ALL)
            for j in range(ND):
                if c >= 1:
                    w_take(v, "dve", 4, NSEG_ALL)
                wte(v, E[("a_sig", c, j)])
                if c > 0 and j == 0:
                    wte(v, E[("a_ugst", c - 1)])
                E[("a_ug", c, j)] = tick("dve", v.scalar_tensor_tensor(
                    ug[:, j * TC:(j + 1) * TC],
                    xnT[:, j * TC:(j + 1) * TC], 1.0, sgslot(j),
                    op0=ALU.mult, op1=ALU.mult,
                    accum_out=usall[:, j * 8 + c: j * 8 + c + 1]))
            E[("a_ugl", c)] = ev("dve")
            wte(v, E[("a_tpl", c * NT + 2)])
            scn = tok_rot[0][:, 0:TC]
            for j in range(ND):
                wte(v, ev("dve"))
                tick("dve", v.tensor_tensor_scan(
                    scn, aac[:, j:j + 1].broadcast_to([P, TC]),
                    ug[:, j * TC:(j + 1) * TC], carr[:, j:j + 1],
                    op0=ALU.mult, op1=ALU.add))
                wte(v, ev("dve"))
                tick("dve", v.tensor_copy(carr[:, j:j + 1],
                                          scn[:, TC - 1:TC]))
            E[("a_carr", c)] = tick("dve", v.tensor_copy(
                carrh[:, (c + 1) * 16:(c + 2) * 16], carr[:]))

        def a_xload_tile(p, g):
            if g >= NCH * NT:
                return
            sft = g % NT
            if g >= 2:
                wte(p, E[("a_xn", g - 2)])
            E[("a_xl", g)] = tick("ax%d" % (sft % 2), p.dma_start(
                out=x_rot[sft % 2][:],
                in_=x_in[(g // NT) * TC + sft * P:
                         (g // NT) * TC + (sft + 1) * P, :]), 16)

        def a_xload(p, c):
            for g in range(c * NT + 2, c * NT + 6):
                a_xload_tile(p, g)

        def a_ugstore(p, c):
            wte(p, E[("a_ugl", c)])
            E[("a_ugst", c)] = tick("ust",
                                    p.dma_start(out=ugs[c], in_=ug[:]), 16)

        def fg_load(s, c, j, hh):
            q = c * 2 * NI + 2 * j + hh
            slot = (2 * j + hh) % 4
            if q >= 4:
                pq = q - 4
                wte(s, E[("c_gmh", pq // (2 * NI),
                          (pq % (2 * NI)) // 2, pq % 2)])
            wte(s, E[("wout", ND + j)])
            E[("c_fgl", c, j, hh)] = tick(
                "fg%d" % slot,
                s.dma_start(out=fgh[slot][:],
                            in_=wgfs[j][:, hh * 1024:(hh + 1) * 1024]), 16)

        def gu_sync(s, c, stop):
            if c == 0:
                for half in range(4):
                    fg_load(s, c, half // 2, half % 2)
            for half in range(4, 2 * NI):
                fg_load(s, c, half // 2, half % 2)

        def wd_load(s, c, j2, hf):
            jj2 = c * ND + j2
            hb = (j2 * 2 + hf) % 4
            if jj2 >= 2:
                wte(s, E[("c_dg", (jj2 - 2) // ND, (jj2 - 2) % ND)])
            if c == 0 and hb == 3:
                wte(s, E["W_spl0"])
                wte(s, E["W_spl1"])
            if c == 0 and jj2 < 2 and hb < 3:
                for sl in range(3 + 4 * hb, 7 + 4 * hb):
                    ql = ((4 * NSEG_ALL - 1 - sl) // NWS) * NWS + sl
                    wte(s, E[("wcq", ql // 4, ql % 4)])
            sb = mats[3]["segbase"]
            wte(s, E[("wout", sb + j2 * 4 + 2 * hf)])
            wte(s, E[("wout", sb + j2 * 4 + 2 * hf + 1)])
            E[("c_wdl", c, j2, hf)] = tick(
                "wd%d" % hb,
                s.dma_start(out=wd_bufs[hb],
                            in_=wdfs[j2][:, hf * 4096:(hf + 1) * 4096]),
                16)

        def gs_store(s, c, j2):
            wte(s, E[("c_gf", c, j2)])
            E[("c_gs", c, j2)] = tick(
                "gs%d" % (j2 % 2),
                s.dma_start(out=gfs[c][:, j2 * TC:(j2 + 1) * TC],
                            in_=gf_rot[j2 % 2][:]), 16)

        def down_sync(s, c):
            if c + 1 < NCH:
                a_wg_load(s, c + 1, 0)
            if c == 0:
                for j2 in (0, 1):
                    for hf in range(2):
                        wd_load(s, c, j2, hf)
            if c + 1 < NCH:
                a_wg_load(s, c + 1, 1)
            for j2 in range(2, ND):
                gs_store(s, c, j2 - 2)
                for hf in range(2):
                    wd_load(s, c, j2, hf)
                if c + 1 < NCH:
                    a_wg_load(s, c + 1, j2)
                if j2 == 4 and c + 1 < NCH:
                    for half in range(4):
                        fg_load(s, c + 1, half // 2, half % 2)
            gs_store(s, c, ND - 2)
            gs_store(s, c, ND - 1)
            if c + 1 < NCH:
                for j2 in (0, 1):
                    for hf in range(2):
                        wd_load(s, c + 1, j2, hf)

        def up_sync(p, c, stop):
            for j in range(NI):
                until = 2 * j + 30
                ti, to = min(until, stop), min(until - 2, stop)
                if wcur["pli"] < min(wcur["plo"] + 2, ti):
                    w_take(p, "pli", min(wcur["plo"] + 2, ti) - wcur["pli"],
                           ti)
                while wcur["pli"] < ti or wcur["plo"] < to:
                    w_take(p, "plo", 1, to)
                    w_take(p, "pli", 1, ti)
                if j == 32:
                    a_xload_tile(p, c * NT + 6)
                    a_xload_tile(p, c * NT + 7)
                jj = c * NI + j
                if jj >= 2:
                    wte(p, E[("c_gu", (jj - 2) // NI, (jj - 2) % NI)])
                wte(p, E[("wout", ND + NI + j)])
                E[("c_wul", c, j)] = tick(
                    "wu%d" % (j % 2),
                    p.dma_start(out=wstr_u[j % 2][:], in_=wufs[j]), 16)
                w_take(p, "pli", 1, stop)
                w_take(p, "plo", 1, stop)
            w_take(p, "pli", 2, stop)
            for _ in range(NSEG_ALL):
                w_take(p, "plo", 1, stop)
                w_take(p, "pli", 1, stop)
            w_take(p, "plo", NSEG_ALL, stop)
            a_xload_tile(p, c * NT + 8)
            a_xload_tile(p, c * NT + 9)

        def gu_pe(t, c, stop):
            for j in range(NI):
                w_take(t, "pe", 12 if (c == 0 and j == 0) else 2, stop)
                gb = (j % 2) * 2
                ub = gb + 1
                if j == 0:
                    wte(t, E[("a_tevl", c)])
                    if c == NCH - 1:
                        wte(t, E["b_rows"])
                if j >= 2:
                    wte(t, E[("c_sg", c, j - 2)])
                else:
                    wte(t, E[("a_sig", c, 12 + gb)])
                for hh in range(2):
                    wte(t, E[("c_fgl", c, j, hh)])
                    for k in range(8 * hh, 8 * hh + 8):
                        ins = t.matmul(
                            psF[gb][:],
                            fgh[(2 * j + hh) % 4][:, (k - 8 * hh) * P:
                                                  (k - 8 * hh + 1) * P],
                            xnT[:, k * TC:(k + 1) * TC],
                            start=(k == 0), stop=(k == ND - 1))
                    E[("c_gmh", c, j, hh)] = tick("pe", ins)
                E[("c_gmm", c, j)] = ("pe", n["pe"])
                wte(t, E[("c_wul", c, j)])
                if j >= 2:
                    wte(t, E[("c_h", c, j - 2)])
                else:
                    wte(t, E[("a_sig", c, 12 + ub)])
                for k in range(ND):
                    ins = t.matmul(psF[ub][:],
                                   wstr_u[j % 2][:, k * P:(k + 1) * P],
                                   xnT[:, k * TC:(k + 1) * TC],
                                   start=(k == 0), stop=(k == ND - 1))
                tick("pe", ins)
                E[("c_umm", c, j)] = ("pe", n["pe"])
                E[("c_gu", c, j)] = ("pe", n["pe"])

        def down_pe(t, c, stop, inserts=None):
            for j2 in range(ND):
                if inserts and j2 in inserts:
                    inserts[j2](t)
                if c + 1 < NCH:
                    if j2 == 0:
                        a_pe_tp(t, c + 1, half=1)
                    else:
                        a_pe_gg(t, c + 1, j2 - 1)
                if c == 0 and j2 == 0:
                    w_take(t, "pe", NSEG_ALL, stop)
                w_take(t, "pe", 4, stop)
                jj2 = c * ND + j2
                db = 4 + j2 % 2
                if j2 == 0:
                    wte(t, E[("c_h", c, NI - 1)])
                if jj2 >= 2:
                    wte(t, E[("c_gf", (jj2 - 2) // ND, (jj2 - 2) % ND)])
                for hf in range(2):
                    wte(t, E[("c_wdl", c, j2, hf)])
                    for kk in range(32):
                        k = hf * 32 + kk
                        ins = t.matmul(
                            psF[db][:],
                            wd_bufs[(j2 * 2 + hf) % 4][:, kk * P:(kk + 1) * P],
                            h[:, k * TC:(k + 1) * TC],
                            start=(k == 0), stop=(k == NI - 1))
                E[("c_dg", c, j2)] = tick("pe", ins)
            if c + 1 < NCH:
                a_pe_gg(t, c + 1, ND - 1)

        def gu_act(a, c, stop):
            for j in range(NI):
                w_take(a, "act", 12 if (c == 0 and j == 0) else 2, stop)
                if c + 1 < NCH:
                    if j == 24:
                        a_act_sqrt(a, c + 1, 0)
                    elif j == 44:
                        a_act_sqrt(a, c + 1, 1)
                jj = c * NI + j
                wte(a, E[("c_gmm", c, j)])
                if jj >= 2:
                    jp = jj - 2
                    wte(a, E[("c_h", jp // NI, jp % NI)])
                E[("c_sg", c, j)] = tick("act", a.activation(
                    sg_rot[j % 2][:], psF[(j % 2) * 2][:], AF.Silu,
                    scale=gscc[:, j:j + 1]))

        def down_act(a, c):
            for j2 in range(ND):
                if c == 0 and j2 == 0:
                    w_take(a, "act", NSEG_ALL, NSEG_ALL)
                w_take(a, "act", 12 if j2 == 0 else 4, NSEG_ALL)
                jj2 = c * ND + j2
                wte(a, E[("c_dg", c, j2)])
                if jj2 >= 2:
                    wte(a, E[("c_gs", (jj2 - 2) // ND, (jj2 - 2) % ND)])
                E[("c_gf", c, j2)] = tick("act", a.activation(
                    gf_rot[j2 % 2][:], psF[4 + j2 % 2][:], AF.Silu,
                    scale=col(CDSC, j2)))
                if c + 1 < NCH and j2 >= 1:
                    a_act_sig(a, c + 1, j2 - 1)
            if c + 1 < NCH:
                a_act_sig(a, c + 1, ND - 1)

        def gu_dve(v, c, stop):
            for j in range(NI):
                w_take(v, "dve", 12 if (c == 0 and j == 0) else 2, stop)
                if c + 1 < NCH:
                    if j == 16:
                        a_ln_stats(v, c + 1, 0)
                    elif j == 28:
                        a_ln_norm(v, c + 1, 0)
                    elif j == 36:
                        a_ln_stats(v, c + 1, 1)
                wte(v, E[("c_umm", c, j)])
                wte(v, E[("c_sg", c, j)])
                if j == 0 and c > 0:
                    wte(v, E[("c_dg", c - 1, ND - 1)])
                E[("c_h", c, j)] = tick("dve", v.scalar_tensor_tensor(
                    h[:, j * TC:(j + 1) * TC],
                    psF[(j % 2) * 2 + 1][:], uscc[:, j:j + 1],
                    sg_rot[j % 2][:],
                    op0=ALU.mult, op1=ALU.mult))

        def b_dve1(v):
            for j in range(ND):
                tick("dve", v.reduce_sum(col(CUST, j),
                                         usall[:, j * 8:(j + 1) * 8], axis=AX))
            E["b_ust"] = ev("dve")

        def b_pe1(t):
            wte(t, E[("a_carr", NCH - 1)])
            wte(t, E["b_ust"])
            wte(t, E[("a_sig", NCH - 1, 12)])
            wte(t, E[("a_sig", NCH - 1, 13)])
            ins = t.transpose(psF[0][0:16, 0:P], carrh[:, 8 * 16: 9 * 16], idf[:])
            E["b_ctp"] = tick("pe", ins)
            ins = t.transpose(psF[1][0:16, 0:P], col(CUST), idf[:])
            E["b_utp"] = tick("pe", ins)

        def b_dve2(v):
            wte(v, E["b_ctp"])
            wte(v, E[("wtp", (NSEG_ALL - 1) * 4 + 3)])
            tick("dve", v.tensor_copy(carrT[:], psF[0][0:16, 0:P]))
            wte(v, E["b_utp"])
            tick("dve", v.tensor_copy(ustotT[:], psF[1][0:16, 0:P]))
            E["b_rows"] = ev("dve")

        def b_sync1(s):
            wte(s, E["b_rows"])
            tick("cci", s.dma_start(
                out=ccin_ap[0:1, :].rearrange("o (j p) -> j (o p)", p=P),
                in_=carrT[:]), 16)
            tick("cci", s.dma_start(
                out=ccin_ap[1:2, :].rearrange("o (j p) -> j (o p)", p=P),
                in_=ustotT[:]), 16)
            E["b_ccin"] = ev("cci")

        def b_cc(p):
            wte(p, E["b_ccin"])
            tick("cc", p.collective_compute(
                "AllGather", ALU.bypass,
                replica_groups=[list(range(8))],
                ins=[ccin_ap.opt()],
                outs=[ccout_ap.opt()]), 1)

        def b_sync2(s):
            s.wait_ge(sems["cc"], 1)
            wte(s, E[("wtp", (NSEG_ALL - 1) * 4 + 3)])
            tick("ccg", s.dma_start(out=gcol, in_=ccout_ap[:, :]), 16)
            tick("ccg", s.dma_start(out=dlrvc[:], in_=dlrv[:]), 16)
            E["b_g"] = ev("ccg")

        def b_pe2(t):
            wte(t, E["b_g"])
            wte(t, E[("c_sg", NCH - 1, 63)])
            wte(t, E[("c_h", NCH - 1, 63)])
            for j in range(ND):
                ins = t.transpose(psF[2][:, j * 16:(j + 1) * 16],
                                  gcol[:, j * P:(j + 1) * P], idf[32:48, 32:48])
                tick("pe", ins)
            E["b_gtp"] = ev("pe")
            for m in range(4):
                ins = t.matmul(psF[3][:, m * 16:(m + 1) * 16], onesr[:],
                               maskc[0:1, m * 16:(m + 1) * 16],
                               start=True, stop=True)
                tick("pe", ins)
            E["b_mbc"] = ev("pe")

        def b_dve3(v):
            wte(v, E["b_gtp"])
            wte(v, E["b_mbc"])
            mbc = sf[:, 0:64]
            tick("dve", v.tensor_copy(mbc, psF[3][:, 0:64]))
            wte(v, ev("dve"))
            for m in range(4):
                for j in range(ND):
                    tick("dve", v.scalar_tensor_tensor(
                        jk16[:], psF[2][:, j * 16:(j + 1) * 16], 1.0,
                        mbc[:, m * 16:(m + 1) * 16],
                        op0=ALU.mult, op1=ALU.mult,
                        accum_out=qm[:, m * 16 + j: m * 16 + j + 1]))
            tick("dve", v.tensor_copy(pwa[:], aac))
            cur, nxt = pwa, pwb
            for step in range(12):
                wte(v, ev("dve"))
                tick("dve", v.tensor_tensor(nxt[:], cur[:], cur[:], op=ALU.mult))
                if step == 8:
                    tick("dve", v.tensor_copy(a512[:], nxt[:]))
                cur, nxt = nxt, cur
            tick("dve", v.tensor_copy(pw1[:], cur[:]))
            wte(v, ev("dve"))
            tick("dve", v.tensor_tensor(pw2[:], pw1[:], pw1[:], op=ALU.mult))
            wte(v, ev("dve"))
            tick("dve", v.tensor_tensor(jk16[:], pw1[:], qm[:, 16:32], op=ALU.mult))
            wte(v, ev("dve"))
            tick("dve", v.tensor_tensor(sin_t[:], qm[:, 0:16], jk16[:], op=ALU.add))
            wte(v, ev("dve"))
            tick("dve", v.tensor_tensor(jk16[:], pw2[:], qm[:, 32:48], op=ALU.mult))
            wte(v, ev("dve"))
            tick("dve", v.tensor_tensor(sin_t[:], sin_t[:], jk16[:], op=ALU.add))
            tick("dve", v.tensor_scalar(col(CUM), qm[:, 48:64], 1.0 / 16384.0,
                                        None, op0=ALU.mult))
            E["b_umean"] = ev("dve")

        def b_pe3(t):
            wte(t, E["b_umean"])
            for k in range(ND):
                ins = t.matmul(psF[0][0:4, 0:1], dlruc[:, k * 4:(k + 1) * 4],
                               col(CUM, k), start=(k == 0), stop=(k == ND - 1))
            E["b_r"] = tick("pe", ins)

        def b_dve4(v):
            wte(v, E["b_r"])
            E["b_rs"] = tick("dve", v.tensor_copy(rsb[:], psF[0][0:4, 0:1]))

        def b_pe4(t):
            wte(t, E["b_rs"])
            for j in range(ND):
                ins = t.matmul(psF[1][:, j:j + 1],
                               dlrvc[0:4, j * P:(j + 1) * P], rsb[:],
                               start=True, stop=True)
                tick("pe", ins)
            E["b_dlr"] = ev("pe")

        def b_dve5(v):
            wte(v, E["b_dlr"])
            tick("dve", v.tensor_scalar(col(CDLR), psF[1][:, 0:16], 0.25, None,
                                        op0=ALU.mult))
            wte(v, ev("dve"))
            tick("dve", v.tensor_tensor(col(CDEFF), col(CD), col(CDLR),
                                        op=ALU.add))
            tick("dve", v.tensor_copy(cura[:], sin_t[:]))
            cur, nxt = cura, curb
            for c in range(NCH):
                wte(v, ev("dve"))
                tick("dve", v.tensor_tensor(sf[:, c * 16:(c + 1) * 16],
                                            carrh[:, c * 16:(c + 1) * 16],
                                            cur[:], op=ALU.add))
                tick("dve", v.tensor_tensor(nxt[:], cur[:], a512[:], op=ALU.mult))
                cur, nxt = nxt, cur
            E["b_done"] = ev("dve")

        HL = ND * TC // 2

        def dslot(q):
            return convbF[:, q * TC:(q + 1) * TC]

        def d2_sync(s, c2):
            if c2 == 0:
                wte(s, E[("a_ugst", NCH - 1)])
                wte(s, E[("a_carr", NCH - 1)])
            else:
                wte(s, E[("d2_dl", c2 - 1, 7)])
            E[("d2_ugh", c2)] = tick("ugl", s.dma_start(
                out=ug[:, 0:HL], in_=ugs[c2][:, 0:HL]), 16)
            if c2 > 0:
                wte(s, E[("d2_dl", c2 - 1, 15)])
            E[("d2_ugl", c2)] = tick("ugl1", s.dma_start(
                out=ug[:, HL:2 * HL], in_=ugs[c2][:, HL:2 * HL]), 16)
            for g in range(4):
                G = c2 * 4 + g
                if G < 2:
                    wte(s, E[("a_xn", 30 + G)])
                else:
                    pg = G - 2
                    wte(s, E[("d2_dl", pg // 4, (pg % 4) * 4 + 3)])
                wte(s, E[("c_gs", c2, g * 4 + 2)])
                wte(s, E[("c_gs", c2, g * 4 + 3)])
                E[("d2_gfl", c2, g)] = tick(
                    "gfl%d" % g,
                    s.dma_start(out=x_rot[g % 2][:],
                                in_=gfs[c2][:, g * D:(g + 1) * D]), 16)

        def d2_dve(v, c2):
            def oev(gg):
                for sft in range(NT):
                    wte(v, E[("d2_tp", c2, gg, sft)])
                    if c2 == 0 and gg < 2:
                        wte(v, E[("a_tpl", 30 + gg)])
                    if E.get(("d2_ost", c2, gg - 2, sft)):
                        wte(v, E[("d2_ost", c2, gg - 2, sft)])
                    elif c2 > 0 and gg < 2:
                        wte(v, E[("d2_ost", c2 - 1, gg + 2, sft)])
                    E[("d2_oev", c2, gg, sft)] = tick("dve", v.tensor_copy(
                        tok_rot[gg % 2][:, sft * TC:(sft + 1) * TC],
                        psF[sft][:]))

            for gg in range(4):
                G = c2 * 4 + gg
                wte(v, E[("d2_gfl", c2, gg)])
                if G >= 1:
                    wte(v, E[("d2_tp", (G - 1) // 4, (G - 1) % 4, NT - 1)])
                for q in range(4):
                    j2 = gg * 4 + q
                    if j2 == 0:
                        wte(v, E[("d2_ugh", c2)])
                        if c2 == 0:
                            wte(v, E["b_done"])
                            wte(v, E[("a_carr", NCH - 1)])
                            wte(v, E[("wtp", (NSEG_ALL - 1) * 4 + 3)])
                            wte(v, E[("c_h", NCH - 1, 62)])
                            wte(v, E[("c_h", NCH - 1, 63)])
                    elif j2 == 8:
                        wte(v, E[("d2_ugl", c2)])
                    s2 = j2 % 2
                    wte(v, ev("dve"))
                    tick("dve", v.tensor_tensor_scan(
                        fghF[s2][:],
                        aac[:, j2:j2 + 1].broadcast_to([P, TC]),
                        ug[:, j2 * TC:(j2 + 1) * TC],
                        sf[:, c2 * 16 + j2: c2 * 16 + j2 + 1],
                        op0=ALU.mult, op1=ALU.add))
                    wte(v, ev("dve"))
                    tick("dve", v.tensor_scalar(
                        y_rot[s2][:], fghF[s2][:], col(CCB, j2),
                        None, op0=ALU.mult))
                    E[("d2_y", c2, j2)] = tick("dve", v.scalar_tensor_tensor(
                        fghF[s2][:], ug[:, j2 * TC:(j2 + 1) * TC],
                        col(CDEFF, j2), y_rot[s2][:],
                        op0=ALU.mult, op1=ALU.add))
                    E[("d2_dl", c2, j2)] = tick("dve", v.tensor_tensor(
                        dslot(q), fghF[s2][:],
                        x_rot[gg % 2][:, q * TC:(q + 1) * TC], op=ALU.mult))
                if gg >= 1:
                    oev(gg - 1)
            oev(3)

        def d2_pe(t, c2):
            for gg in range(4):
                wte(t, E[("d2_dl", c2, gg * 4 + 3)])
                for sft in range(NT):
                    G = c2 * 4 + gg
                    if G >= 1:
                        wte(t, E[("d2_oev", (G - 1) // 4, (G - 1) % 4, sft)])
                    for q in range(4):
                        ins = t.transpose(
                            psF[sft][:, q * P:(q + 1) * P],
                            dslot(q)[:, sft * P:(sft + 1) * P],
                            idf[:])
                    E[("d2_tp", c2, gg, sft)] = tick("pe", ins)
            E[("d2_tpl", c2)] = ev("pe")

        def d2_spl(p, c2):
            if c2 == 0:
                wte(p, E["xpr"])
            for gg in range(4):
                for sft in range(NT):
                    wte(p, E[("d2_oev", c2, gg, sft)])
                    E[("d2_ost", c2, gg, sft)] = tick(
                        "os%d" % ((gg % 2) * 4 + sft), p.dma_start(
                            out=out_d[c2 * TC + sft * P:
                                      c2 * TC + (sft + 1) * P,
                                      gg * TC:(gg + 1) * TC],
                            in_=tok_rot[gg % 2][:, sft * TC:(sft + 1) * TC],
                            accum_op=ALU.add), 16)

        sect("sync", lambda s: a_sync(s, 0))
        sect("gpsimd", lambda p: a_xload(p, 0))

        def a0_act(a):
            a_act_sqrt(a, 0, 0)
            a_act_sqrt(a, 0, 1)
            for j in range(ND):
                a_act_sig(a, 0, j)
        sect("scalar", a0_act)

        def a0_dve(v):
            a_ln_pair(v, 0, 0)
            a_evacs_half(v, 0, 0)
            a_ln_pair(v, 0, 1)
            a_evacs_half(v, 0, 1)
            a_ug_scan(v, 0)
        sect("vector", a0_dve)
        sect("tensor", lambda t: a_pe(t, 0))

        for c in range(NCH):
            stop = W_STOP[c]
            last = (c == NCH - 1)

            sect("sync", lambda s, c=c, stop=stop: gu_sync(s, c, stop))
            if c > 0:
                sect("gpsimd", lambda p, c=c: a_ugstore(p, c))
            sect("gpsimd", lambda p, c=c, stop=stop: up_sync(p, c, stop))
            if c == 0:
                sect("gpsimd", lambda p: a_ugstore(p, 0))
            if last:
                sect("vector", b_dve1)
                sect("tensor", b_pe1)
                sect("vector", b_dve2)
            sect("scalar", lambda a, c=c, stop=stop: gu_act(a, c, stop))
            sect("vector", lambda v, c=c, stop=stop: gu_dve(v, c, stop))
            sect("tensor", lambda t, c=c, stop=stop: gu_pe(t, c, stop))

            if not last:
                sect("tensor", lambda t, c=c: a_pe_tp(t, c + 1, half=0))
                sect("vector", lambda v, c=c: (a_evacs_half(v, c + 1, 0),
                                               a_ln_norm(v, c + 1, 1),
                                               a_evacs_half(v, c + 1, 1),
                                               a_ug_scan(v, c + 1)))
                sect("sync", lambda s, c=c: down_sync(s, c))
                sect("tensor", lambda t, c=c, stop=stop: down_pe(t, c, stop))
                sect("scalar", lambda a, c=c: down_act(a, c))
            else:
                sect("gpsimd", b_sync1)
                sect("gpsimd", b_cc)
                sect("gpsimd", b_sync2)
                sect("vector", b_dve3)
                sect("vector", b_dve4)
                sect("vector", b_dve5)
                sect("sync", lambda s, c=c: down_sync(s, c))
                sect("tensor", lambda t, c=c, stop=stop: down_pe(
                    t, c, stop, inserts={6: b_pe2, 10: b_pe3, 13: b_pe4}))
                sect("scalar", lambda a, c=c: down_act(a, c))

        for c2 in range(NCH):
            sect("sync", lambda s, c2=c2: d2_sync(s, c2))
            sect("tensor", lambda t, c2=c2: d2_pe(t, c2))
            sect("vector", lambda v, c2=c2: d2_dve(v, c2))
            sect("gpsimd", lambda p, c2=c2: d2_spl(p, c2))

        def tail(p):
            for k in ("os0", "os1", "os2", "os3", "os4", "os5", "os6", "os7",
                      "ust", "wo0", "wo1", "xpr", "gs0", "gs1"):
                p.wait_ge(sems[k], n[k])
        sect("gpsimd", tail)

        stub = _StubEngine()
        for k in ("dve", "act", "pe", "pli", "plo"):
            wcur[k] = 0
        for engname, fn in sections:
            fn(stub)
        pass1 = dict(n)
        pass1_wcur = dict(wcur)
        assert all(v == NSEG_ALL for v in pass1_wcur.values()), pass1_wcur
        for k in n:
            n[k] = 0
        for k in wcur:
            wcur[k] = 0
        E.pass2 = True
        for engname, fn in sections:
            getattr(block, engname)(fn)
        assert n == pass1, f"two-pass mismatch: {n} vs {pass1}"
        assert wcur == pass1_wcur, f"wcur mismatch: {wcur} vs {pass1_wcur}"

    return nc


_NC_CACHE = None


def kernel(**inputs):
    global _NC_CACHE
    x = np.ascontiguousarray(inputs["L0_attn_u"])
    B, T, Dm = x.shape
    assert (B, T, Dm) == (2, 16384, 2048)

    def row(v):
        return np.ascontiguousarray(np.asarray(v, dtype=np.float32).reshape(1, -1))

    shared = dict(
        av=row(inputs["L0_attn_a"]),
        bv=row(inputs["L0_attn_b"]),
        cv=row(inputs["L0_attn_c"]),
        dv=row(inputs["L0_attn_d"]),
        gbv=row(inputs["L0_attn_gate_b"]),
        dscv=row(inputs["L0_ffn_down_scale"]),
        gscv=row(inputs["L0_ffn_gate_scale"]),
        uscv=row(inputs["L0_ffn_up_scale"]),
        gate_w=np.ascontiguousarray(inputs["L0_attn_gate_w"], dtype=np.float32),
        gq=np.ascontiguousarray(inputs["L0_ffn_gate_w"], dtype=np.int32),
        uq=np.ascontiguousarray(inputs["L0_ffn_up_w"], dtype=np.int32),
        dq=np.ascontiguousarray(inputs["L0_ffn_down_w"], dtype=np.int32),
        dlru=np.ascontiguousarray(inputs["L0_attn_d_lr_u"], dtype=np.float32),
        dlrv=np.ascontiguousarray(inputs["L0_attn_d_lr_v"], dtype=np.float32),
        identf=np.eye(P, dtype=np.float32),
        identb=np.eye(P, dtype=np.float32).astype(ml_dtypes.bfloat16),
    )

    in_maps = []
    for core in range(8):
        b, q = core // 4, core % 4
        msk = np.zeros((1, 64), dtype=np.float32)
        for j in range(4):
            m = q - 1 - j
            if 0 <= m <= 2:
                msk[0, m * 16 + 2 * (4 * b + j)] = 1.0
            msk[0, 48 + 2 * (4 * b + j) + 1] = 1.0
        im = dict(shared)
        im["x"] = np.ascontiguousarray(x[b, q * TCORE:(q + 1) * TCORE, :])
        im["masks"] = msk
        in_maps.append(im)

    if _NC_CACHE is None:
        _NC_CACHE = build_nc()
    res = run_bass_kernel_spmd(_NC_CACHE, in_maps, core_ids=list(range(8)))

    out = np.empty((B, T, Dm), dtype=np.float32)
    for core in range(8):
        b, q = core // 4, core % 4
        out[b, q * TCORE:(q + 1) * TCORE, :] = res.results[core]["out"]
    return out
